# version 13
# speedup vs baseline: 1.1774x; 1.0006x over previous
import numpy as np
import ml_dtypes
import concourse.bass as bass
import concourse.mybir as mybir
from concourse.bass_utils import run_bass_kernel_spmd

F32 = mybir.dt.float32
BF16 = mybir.dt.bfloat16
AX = mybir.AxisListType.X
ALU = mybir.AluOpType
ACT = mybir.ActivationFunctionType

N = 2048
C = 2048
NT = 16
NCH = 16
NE = 8

SEMS = ("tS", "vS", "aS", "vD", "gS",
        "dX0", "dX1", "dX2", "dX3", "dW0", "dW1", "dW2", "dW3", "dW4",
        "dT0", "dT1", "dT2", "dT3", "dT4")
DMA_SEMS = SEMS[5:]


class Ctx:
    def __init__(self):
        self.cnt = {s: 0 for s in SEMS}
        self.ev = {}
        self.emit = False
        self.sems = None

    def inc(self, instr, sem, k=1):
        self.cnt[sem] += k
        if self.emit:
            instr.then_inc(self.sems[sem], k)
        return self.cnt[sem]

    def mark(self, key):
        if not self.emit:
            assert key not in self.ev, key
            self.ev[key] = dict(self.cnt)

    def val(self, key, sem):
        assert key in self.ev, f"missing event {key}"
        return self.ev[key][sem]

    def wait(self, eng, key, sem):
        if self.emit:
            v = self.val(key, sem)
            if v > 0:
                eng.wait_ge(self.sems[sem], v)

    def D(self, v, instr):
        self.cnt["vD"] += 1
        if self.emit:
            instr.then_inc(self.sems["vD"], 1)
            v.wait_ge(self.sems["vD"], self.cnt["vD"])

    def adrain(self, s):
        self.cnt["aS"] += 1
        if self.emit:
            s.drain()
            s.nop().then_inc(self.sems["aS"], 1)


class MockInstr:
    def then_inc(self, *a, **k):
        return self


class MockEng:
    def __getattr__(self, name):
        def f(*a, **k):
            return MockInstr()
        return f


def build_program(upto=6):
    nc = bass.Bass("TRN2", target_bir_lowering=False)

    x_h = nc.dram_tensor("x", [N, C], F32, kind="ExternalInput").ap()
    wqkv_h = nc.dram_tensor("wqkv", [3 * C, C], F32, kind="ExternalInput").ap()
    wproj_h = nc.dram_tensor("wproj", [C, C], F32, kind="ExternalInput").ap()
    bpt_h = nc.dram_tensor("bpt", [128, C], F32, kind="ExternalInput").ap()
    rw_h = nc.dram_tensor("rw", [C, 8], F32, kind="ExternalInput").ap()
    rbt_h = nc.dram_tensor("rbt", [128, 8], F32, kind="ExternalInput").ap()
    ew_h = nc.dram_tensor("ew", [NE, C, C], F32, kind="ExternalInput").ap()
    ebt_h = nc.dram_tensor("ebt", [NE, 128, C], F32, kind="ExternalInput").ap()
    pw_h = nc.dram_tensor("pw", [C, C], F32, kind="ExternalInput").ap()
    pbt_h = nc.dram_tensor("pbt", [128, C], F32, kind="ExternalInput").ap()
    nwt_h = nc.dram_tensor("nwt", [128, C], F32, kind="ExternalInput").ap()
    i32_h = nc.dram_tensor("ind32", [32, 128], F32, kind="ExternalInput").ap()
    cpad_h = nc.dram_tensor("cpad", [128, 128], F32, kind="ExternalInput").ap()
    out_h = nc.dram_tensor("out", [N, C], F32, kind="ExternalOutput").ap()

    import contextlib
    ctx = contextlib.ExitStack()
    RB = ctx.enter_context(nc.sbuf_tensor("RB", [128, 92256], BF16))
    RF = ctx.enter_context(nc.sbuf_tensor("RF", [128, 6656], F32))
    RBa = RB[:, :]
    RFa = RF[:, :]

    W0 = 65536
    xn = RBa[:, 0:32768]
    combT = RBa[:, 0:32768]
    big = RBa[:, 32768:65536]
    pwq = RBa[:, 32768:65536]
    wk_t = RBa[:, W0 + 0 : W0 + 4096]
    wv_t = RBa[:, W0 + 4096 : W0 + 8192]
    k_s = RBa[:, W0 + 8192 : W0 + 12288]
    v_s = RBa[:, W0 + 12288 : W0 + 16384]
    wpT0 = RBa[:, W0 + 0 : W0 + 4096]
    wpT1 = RBa[:, W0 + 4096 : W0 + 8192]
    wstg = RBa[:, W0 + 8192 : W0 + 12288]
    wqnat = RBa[:, W0 + 12288 : W0 + 16384]
    wqT = RBa[:, W0 + 16384 : W0 + 20480]
    qb = RBa[:, W0 + 20480 : W0 + 24576]
    eq = RBa[:, W0 + 0 : W0 + 16384]
    ebb = RBa[:, W0 + 16384 : W0 + 20480]
    sdump = RBa[:, W0 + 0 : W0 + 2048]
    srw = RBa[:, W0 + 2048 : W0 + 2176]
    sdump2 = RBa[:, W0 + 4096 : W0 + 6144]
    pbb = RBa[:, W0 + 0 : W0 + 2048]
    snwb = RBa[:, W0 + 2048 : W0 + 4096]
    cpad_s = RBa[:, W0 + 20480 : W0 + 20608]
    ones32 = RBa[:, 90112:90144]
    sdenm = RBa[:, 90144:90208]
    vkbd = RBa[:, 90208:92256]
    xst = RFa[:, 0:4096]
    xsl = RFa[:, 0:1024]
    oev = RFa[:, 4096:5120]
    rden = RFa[:, 1024:3072]
    bcsb = RFa[:, 3072:4096]
    sbpjb = RFa[:, 1024:3072]
    silu = RFa[:, 0:1024]
    lgall = RFa[:, 0:128]
    hbuf = RFa[:, 1024:2048]
    o1rd = RFa[:, 2048:4096]
    ybuf = RFa[:, 4096:6144]
    smean = RFa[:, 6144:6160]
    srstd = RFa[:, 6160:6176]
    snegmr = RFa[:, 6176:6192]
    sfw = RFa[:, 6192:6320]
    scst = RFa[:, 6320:6336]
    srbb = RFa[:, 6336:6344]
    stats = RFa[:, 6344:6424]
    ind32 = RFa[0:32, 6424:6552]

    C_ZERO, C_ONE, C_N1E30, C_EPSLN, C_EPSRMS, C_1E15, C_NEG1, C_NEG05, C_I2048 = range(9)

    pf = [ctx.enter_context(nc.psum_tensor(f"pf{i}", [128, 512], F32)) for i in range(8)]

    sems = {s: ctx.enter_context(nc.semaphore(s)) for s in SEMS}


    def setup_sync(e, p):
        p.inc(e.dma_start(out=ind32[:, :], in_=i32_h[:, :]), "dX0", 16)
        p.inc(e.dma_start(out=srbb[:, :], in_=rbt_h[:, :]), "dX0", 16)

    def setup_gp(e, p):
        for j in range(16):
            p.inc(e.dma_start(out=srw[:, 8 * j : 8 * j + 8],
                              in_=rw_h[128 * j : 128 * (j + 1), :]), "dW0", 16)

    def setup_vec(e, p):
        vals = [0.0, 1.0, -1e30, 1e-5, 1e-6, 1e-15, -1.0, -0.5, 1.0 / 2048.0]
        for i, val in enumerate(vals):
            e.memset(scst[:, i : i + 1], val)
        e.memset(ones32[:, :], 1.0)
        e.memset(vkbd[:, :], 0.0)
        p.D(e, e.memset(sdenm[:, :], 0.0))

    def A_sync(e, p):
        for j in range(2):
            p.inc(e.dma_start(out=xst[:, 2048 * j : 2048 * j + 2048],
                              in_=x_h[128 * j : 128 * (j + 1), :]), f"dX{j % 2}", 16)
            p.mark(f"a_x_{j}")
        for j in range(NT):
            p.wait(e, f"a_xn_{j}", "vS")
            src = wqnat[:, 2048 * (j % 2) : 2048 * (j % 2) + 2048]
            dst = xn[:, :].rearrange("p (cc t) -> p cc t", t=2048)[:, :, 128 * j : 128 * j + 128]
            p.inc(e.dma_start_transpose(out=dst, in_=src), f"dT{j % 2}", 16)
            p.mark(f"a_tp_{j}")
            if j + 2 < NT:
                p.inc(e.dma_start(out=xst[:, 2048 * ((j + 2) % 2) : 2048 * ((j + 2) % 2) + 2048],
                                  in_=x_h[128 * (j + 2) : 128 * (j + 3), :]), f"dX{j % 2}", 16)
                p.mark(f"a_x_{j+2}")

    def A_scalar(e, p):
        for j in range(NT):
            xt = xst[:, 2048 * (j % 2) : 2048 * (j % 2) + 2048]
            p.wait(e, f"a_x_{j}", f"dX{j % 2}")
            p.inc(e.activation(sdump[:, :], xt, ACT.Square, accum_out=stats[:, 0:1]), "aS")
            p.mark(f"a_ssq_{j}")
            p.wait(e, f"a_var_{j}", "vS")
            e.activation(stats[:, 4:5], stats[:, 3:4], ACT.Ln, bias=scst[:, C_EPSLN : C_EPSLN + 1])
            if p.emit:
                e.drain()
            e.activation(srstd[:, j : j + 1], stats[:, 4:5], ACT.Exp,
                         scale=scst[:, C_NEG05 : C_NEG05 + 1])
            p.adrain(e)
            p.mark(f"a_rstd_{j}")

    def A_vector(e, p):
        def wchain(j):
            RB = 8 + 36 * (j % 2)
            p.wait(e, f"a_exp_{j}", "aS")
            p.D(e, e.tensor_scalar(stats[:, RB + 28 : RB + 29], stats[:, RB + 27 : RB + 28],
                                   scst[:, C_ONE : C_ONE + 1],
                                   scst[:, C_ONE : C_ONE + 1], ALU.mult, ALU.add))
            p.D(e, e.reciprocal(stats[:, RB + 29 : RB + 30], stats[:, RB + 28 : RB + 29]))
            p.D(e, e.tensor_tensor(stats[:, RB + 30 : RB + 31], stats[:, RB + 27 : RB + 28],
                                   stats[:, RB + 29 : RB + 30], ALU.mult))
            p.D(e, e.tensor_scalar(sfw[:, 8 * j : 8 * j + 8], stats[:, RB + 8 : RB + 16],
                                   stats[:, RB + 29 : RB + 30],
                                   scst[:, C_ZERO : C_ZERO + 1], ALU.mult, ALU.add))
            p.inc(e.scalar_tensor_tensor(sfw[:, 8 * j : 8 * j + 8], stats[:, RB + 16 : RB + 24],
                                          stats[:, RB + 30 : RB + 31],
                                          sfw[:, 8 * j : 8 * j + 8], ALU.mult, ALU.add), "vS")
            p.mark(f"a_top2_{j}")

        for j in range(NT):
            xt = xst[:, 2048 * (j % 2) : 2048 * (j % 2) + 2048]
            p.wait(e, f"a_x_{j}", f"dX{j % 2}")
            p.D(e, e.reduce_sum(stats[:, 1:2], xt, axis=AX))
            p.wait(e, f"a_ssq_{j}", "aS")
            p.D(e, e.tensor_scalar(smean[:, j : j + 1], stats[:, 1:2], scst[:, C_I2048 : C_I2048 + 1],
                                   scst[:, C_ZERO : C_ZERO + 1], ALU.mult, ALU.add))
            p.D(e, e.tensor_scalar(stats[:, 2:3], stats[:, 0:1], scst[:, C_I2048 : C_I2048 + 1],
                                   scst[:, C_ZERO : C_ZERO + 1], ALU.mult, ALU.add))
            p.D(e, e.tensor_tensor(stats[:, 5:6], smean[:, j : j + 1], smean[:, j : j + 1], ALU.mult))
            p.inc(e.tensor_tensor(stats[:, 3:4], stats[:, 2:3], stats[:, 5:6], ALU.subtract), "vS")
            p.mark(f"a_var_{j}")
            p.wait(e, f"a_rstd_{j}", "aS")
            p.D(e, e.tensor_tensor(stats[:, 6:7], smean[:, j : j + 1], srstd[:, j : j + 1], ALU.mult))
            p.D(e, e.tensor_scalar(snegmr[:, j : j + 1], stats[:, 6:7], -1.0, None, ALU.mult))
            xbf = wqnat[:, 2048 * (j % 2) : 2048 * (j % 2) + 2048]
            if j >= 2:
                p.wait(e, f"a_tp_{j-2}", f"dT{j % 2}")
            p.inc(e.tensor_scalar(xbf, xt, srstd[:, j : j + 1], snegmr[:, j : j + 1],
                                  ALU.mult, ALU.add), "vS")
            p.mark(f"a_xn_{j}")

    def A_tensor(e, p):
        for j in range(NT):
            p.wait(e, f"a_tp_{j}", f"dT{j % 2}")
            for cc in range(NCH):
                ins = e.matmul(pf[0][:, 8 * j : 8 * j + 8],
                               xn[:, cc * 2048 + 128 * j : cc * 2048 + 128 * j + 128],
                               srw[:, 8 * cc : 8 * cc + 8], start=(cc == 0), stop=(cc == 15))
            p.inc(ins, "tS")
            p.mark(f"a_rt_{j}")

    def Bkv_gp(e, p):
        for s in range(8):
            for kv in range(2):
                slot = (2 * s + kv) % 4
                if 2 * s + kv >= 4:
                    pidx = 2 * s + kv - 4
                    ps_, pkv = pidx // 2, pidx % 2
                    p.wait(e, f"bkv_tp_{ps_}_{pkv}", f"dT{2 * pkv + ps_ % 2}")
                src = wqkv_h[C * (1 + kv) + 256 * s : C * (1 + kv) + 256 * (s + 1), :]
                src = src.rearrange("(oc p) c -> p oc c", p=128)
                dst = big[:, 4096 * slot : 4096 * (slot + 1)].rearrange("p (oc c) -> p oc c", oc=2)
                p.inc(e.dma_start(out=dst, in_=src), f"dW{slot}", 16)
                p.mark(f"bkv_w_{s}_{kv}")
        for j in range(2):
            p.inc(e.dma_start(out=big[:, 20480 + 2048 * j : 20480 + 2048 * j + 2048],
                              in_=wqkv_h[128 * j : 128 * (j + 1), :]), "dW4", 16)
            p.mark(f"batt_wq_{j}")
        for cs in range(2):
            for g in range(4):
                p.wait(e, "bkv_tp_7_1", f"dT{g}")
            src = wproj_h[256 * cs : 256 * (cs + 1), :].rearrange("(oc p) c -> p oc c", p=128)
            dst = big[:, 4096 * cs : 4096 * (cs + 1)].rearrange("p (oc c) -> p oc c", oc=2)
            p.inc(e.dma_start(out=dst, in_=src), f"dW{cs}", 16)
            p.mark(f"bkv_wp_{cs}")

    def Bkv_sync(e, p):
        for s in range(8):
            for kv in range(2):
                slot = (2 * s + kv) % 4
                stage = big[:, 4096 * slot : 4096 * (slot + 1)]
                if kv == 0:
                    wt = wk_t if s % 2 == 0 else wqT
                else:
                    wt = wv_t if s % 2 == 0 else qb
                p.wait(e, f"bkv_w_{s}_{kv}", f"dW{slot}")
                if s >= 2:
                    p.wait(e, f"bkv_kvmm_{s-2}", "tS")
                for oc in range(2):
                    src = stage[:, 2048 * oc : 2048 * (oc + 1)]
                    dst = wt[:, :].rearrange("p (cc o) -> p cc o", o=256)[:, :, 128 * oc : 128 * oc + 128]
                    p.inc(e.dma_start_transpose(out=dst, in_=src), f"dT{2 * kv + s % 2}", 16)
                p.mark(f"bkv_tp_{s}_{kv}")
        for j in range(2):
            p.wait(e, "batt_wq_1", "dW4")
            src = big[:, 20480 + 2048 * j : 20480 + 2048 * j + 2048]
            dst = big[:, 16384 + 2048 * j : 16384 + 2048 * j + 2048].rearrange(
                "p (cc m) -> p cc m", m=128)
            p.inc(e.dma_start_transpose(out=dst, in_=src), "dT4", 16)
            p.mark(f"batt_tp_{j}")
        for cs in range(2):
            p.wait(e, f"bkv_wp_{cs}", f"dW{cs}")
            p.wait(e, "bkv_kvmm_7", "tS")
            stage = big[:, 4096 * cs : 4096 * (cs + 1)]
            wt = wpT0 if cs == 0 else wpT1
            for oc in range(2):
                src = stage[:, 2048 * oc : 2048 * (oc + 1)]
                dst = wt[:, :].rearrange("p (cc o) -> p cc o", o=256)[:, :, 128 * oc : 128 * oc + 128]
                p.inc(e.dma_start_transpose(out=dst, in_=src), "dT2", 16)
            p.mark(f"bkv_wptp_{cs}")

    def Bkv_tensor(e, p):
        p.wait(e, "lg_copied", "vS")
        for s in range(8):
            p.wait(e, f"bkv_tp_{s}_1", f"dT{s % 2}")
            p.wait(e, f"bkv_tp_{s}_1", f"dT{2 + s % 2}")
            for n in range(16):
                for kv in range(2):
                    gk = 16 * s + n
                    bank = pf[kv * 2 + n % 2]
                    if gk >= 2:
                        ps_, pn = divmod(gk - 2, 16)
                        if kv == 0:
                            p.wait(e, f"bkv_kev_{ps_}_{pn}", "aS")
                        else:
                            p.wait(e, f"bkv_vev_{ps_}_{pn}", "vS")
                    if kv == 0:
                        wt = wk_t if s % 2 == 0 else wqT
                    else:
                        wt = wv_t if s % 2 == 0 else qb
                    for cc in range(NCH):
                        ins = e.matmul(bank[:, 0:256],
                                       xn[:, cc * 2048 + 128 * n : cc * 2048 + 128 * n + 128],
                                       wt[:, 256 * cc : 256 * (cc + 1)], start=(cc == 0), stop=(cc == 15))
                    p.inc(ins, "tS")
                    p.mark(f"bkv_{'k' if kv == 0 else 'v'}mm_{s}_{n}")
            p.mark(f"bkv_kvmm_{s}")
            p.wait(e, f"bkv_kev_{s}_15", "aS")
            p.wait(e, f"bkv_vev_{s}_15", "vS")
            if s > 0:
                p.wait(e, f"bkv_vke_{s-1}", "vS")
            for g in range(2):
                jg = 2 * s + g
                for n in range(16):
                    e.matmul(pf[4][:, 128 * g : 128 * g + 128],
                             k_s[:, 256 * n + 128 * g : 256 * n + 128 * g + 128],
                             v_s[:, 256 * n + 128 * g : 256 * n + 128 * g + 128],
                             start=(n == 0), stop=(n == 15))
                for n in range(16):
                    ins = e.matmul(pf[5][:, 32 * jg : 32 * jg + 32],
                                   k_s[:, 256 * n + 128 * g : 256 * n + 128 * g + 128],
                                   ones32[:, 0:32], start=(n == 0), stop=(n == 15))
            p.inc(ins, "tS")
            p.mark(f"bkv_vk_{s}")

    def wchain_bkv(e, p, j):
        RB = 8 + 36 * (j % 2)
        p.wait(e, f"a_exp_{j}", "aS")
        p.D(e, e.tensor_scalar(stats[:, RB + 28 : RB + 29], stats[:, RB + 27 : RB + 28],
                               scst[:, C_ONE : C_ONE + 1],
                               scst[:, C_ONE : C_ONE + 1], ALU.mult, ALU.add))
        p.D(e, e.reciprocal(stats[:, RB + 29 : RB + 30], stats[:, RB + 28 : RB + 29]))
        p.D(e, e.tensor_tensor(stats[:, RB + 30 : RB + 31], stats[:, RB + 27 : RB + 28],
                               stats[:, RB + 29 : RB + 30], ALU.mult))
        p.D(e, e.tensor_scalar(sfw[:, 8 * j : 8 * j + 8], stats[:, RB + 8 : RB + 16],
                               stats[:, RB + 29 : RB + 30],
                               scst[:, C_ZERO : C_ZERO + 1], ALU.mult, ALU.add))
        p.D(e, e.scalar_tensor_tensor(sfw[:, 8 * j : 8 * j + 8], stats[:, RB + 16 : RB + 24],
                                      stats[:, RB + 30 : RB + 31],
                                      sfw[:, 8 * j : 8 * j + 8], ALU.mult, ALU.add))
        p.inc(e.nop(), "vS")
        p.mark(f"a_top2_{j}")

    def Bkv_scalar(e, p):
        def expd(j_):
            RB = 8 + 36 * (j_ % 2)
            p.wait(e, f"a_d_{j_}", "vS")
            e.activation(stats[:, RB + 27 : RB + 28], stats[:, RB + 26 : RB + 27], ACT.Exp)
            p.adrain(e)
            p.mark(f"a_exp_{j_}")

        for s in range(8):
            for n in range(16):
                p.wait(e, f"bkv_kmm_{s}_{n}", "tS")
                p.inc(e.activation(k_s[:, 256 * n : 256 * (n + 1)], pf[n % 2][:, 0:256], ACT.Relu),
                      "aS")
                p.mark(f"bkv_kev_{s}_{n}")
            if s >= 1:
                expd(2 * s - 2)
                expd(2 * s - 1)
        expd(14)
        expd(15)

    def Bkv_vector(e, p):
        p.wait(e, f"a_rt_{NT-1}", "tS")
        p.inc(e.tensor_copy(lgall[:, :], pf[0][:, 0:128]), "vS")
        p.mark("lg_copied")
        for s in range(8):
            for n in range(16):
                p.wait(e, f"bkv_vmm_{s}_{n}", "tS")
                p.inc(e.tensor_copy(v_s[:, 256 * n : 256 * (n + 1)], pf[2 + n % 2][:, 0:256]), "vS")
                p.mark(f"bkv_vev_{s}_{n}")
            p.wait(e, f"bkv_vk_{s}", "tS")
            for g in range(2):
                jg = 2 * s + g
                for i in range(4):
                    e.tensor_copy(vkbd[32 * i : 32 * i + 32, 128 * jg + 32 * i : 128 * jg + 32 * i + 32],
                                  pf[4][32 * i : 32 * i + 32, 128 * g + 32 * i : 128 * g + 32 * i + 32])
                    ins = e.tensor_copy(sdenm[32 * i : 32 * i + 32, 4 * jg + i : 4 * jg + i + 1],
                                        pf[5][32 * i : 32 * i + 32, 32 * jg : 32 * jg + 1])
            p.inc(ins, "vS")
            p.mark(f"bkv_vke_{s}")
            if s >= 1:
                wchain_bkv(e, p, 2 * s - 2)
                wchain_bkv(e, p, 2 * s - 1)
            for j in (2 * s, 2 * s + 1):
                RB = 8 + 36 * (j % 2)
                p.D(e, e.tensor_tensor(stats[:, RB : RB + 8], lgall[:, 8 * j : 8 * j + 8],
                                       srbb[:, 0:8], ALU.add))
                p.D(e, e.reduce_max(stats[:, RB + 24 : RB + 25], stats[:, RB : RB + 8], axis=AX))
                p.D(e, e.tensor_scalar(stats[:, RB + 8 : RB + 16], stats[:, RB : RB + 8],
                                       stats[:, RB + 24 : RB + 25],
                                       scst[:, C_ONE : C_ONE + 1], ALU.is_equal, ALU.mult))
                p.D(e, e.scalar_tensor_tensor(stats[:, RB + 16 : RB + 24], stats[:, RB + 8 : RB + 16],
                                              scst[:, C_N1E30 : C_N1E30 + 1],
                                              stats[:, RB : RB + 8], ALU.mult, ALU.add))
                p.D(e, e.reduce_max(stats[:, RB + 25 : RB + 26], stats[:, RB + 16 : RB + 24], axis=AX))
                p.D(e, e.tensor_scalar(stats[:, RB + 16 : RB + 24], stats[:, RB + 16 : RB + 24],
                                       stats[:, RB + 25 : RB + 26],
                                       scst[:, C_ONE : C_ONE + 1], ALU.is_equal, ALU.mult))
                p.inc(e.tensor_tensor(stats[:, RB + 26 : RB + 27], stats[:, RB + 25 : RB + 26],
                                      stats[:, RB + 24 : RB + 25], ALU.subtract), "vS")
                p.mark(f"a_d_{j}")
        wchain_bkv(e, p, 14)
        wchain_bkv(e, p, 15)

    def Batt_gp(e, p):
        for j in range(2, NT):
            if j >= 4:
                p.wait(e, f"batt_tp_{j-2}", f"dT{j % 2}")
            p.inc(e.dma_start(out=wqnat[:, 2048 * (j % 2) : 2048 * (j % 2) + 2048],
                              in_=wqkv_h[128 * j : 128 * (j + 1), :]), f"dW{j % 2}", 16)
            p.mark(f"batt_wq_{j}")
        p.wait(e, f"batt_amix_{4*NT-1}", "vS")
        p.inc(e.dma_start(out=sbpjb[:, :], in_=bpt_h[:, :]), "dW3", 16)
        p.mark("batt_bp")
        for cs in range(2, 8):
            if cs >= 3:
                p.wait(e, f"batt_wptp_{cs-1}", "dT2")
            src = wproj_h[256 * cs : 256 * (cs + 1), :].rearrange("(oc p) c -> p oc c", p=128)
            dst = wstg[:, :].rearrange("p (oc c) -> p oc c", oc=2)
            p.inc(e.dma_start(out=dst, in_=src), "dW2", 16)
            p.mark(f"batt_wp_{cs}")

    def Batt_sync(e, p):
        for j in range(2, NT):
            p.wait(e, f"batt_wq_{j}", f"dW{j % 2}")
            if j >= 4:
                p.wait(e, f"batt_qmm_{j-2}_3", "tS")
            src = wqnat[:, 2048 * (j % 2) : 2048 * (j % 2) + 2048]
            dst = wqT[:, 2048 * (j % 2) : 2048 * (j % 2) + 2048]
            dst = dst.rearrange("p (cc m) -> p cc m", m=128)
            p.inc(e.dma_start_transpose(out=dst, in_=src), f"dT{j % 2}", 16)
            p.mark(f"batt_tp_{j}")
        for cs in range(8):
            if cs >= 2:
                p.wait(e, f"batt_wp_{cs}", "dW2")
                p.wait(e, f"batt_pmm_{cs-2}_15", "tS")
                wt = wpT0 if cs % 2 == 0 else wpT1
                for oc in range(2):
                    src = wstg[:, 2048 * oc : 2048 * (oc + 1)]
                    dst = wt[:, :].rearrange("p (cc o) -> p cc o", o=256)[:, :, 128 * oc : 128 * oc + 128]
                    p.inc(e.dma_start_transpose(out=dst, in_=src), "dT2", 16)
                p.mark(f"batt_wptp_{cs}")
            for n in range(16):
                pi = 16 * cs + n
                p.wait(e, f"batt_o1_{pi}", "vS")
                p.inc(e.dma_start(out=out_h[128 * n : 128 * (n + 1), 256 * cs : 256 * (cs + 1)],
                                  in_=oev[:, 256 * (pi % 4) : 256 * (pi % 4) + 256]),
                      f"dX{2 + pi % 2}", 16)
                p.mark(f"batt_o1w_{pi}")

    def Batt_tensor(e, p):
        def q_mm(j):
            if j < 2:
                p.wait(e, "batt_tp_1", "dT4")
            else:
                p.wait(e, f"batt_tp_{j}", f"dT{j % 2}")
            wqsrc = big[:, 16384 : 16384 + 4096] if j < 2 else wqT
            wqoff = 2048 * (j % 2) if j >= 2 else 2048 * j
            for ns in range(4):
                qi = 4 * j + ns
                if qi >= 2:
                    p.wait(e, f"batt_qev_{qi-2}", "aS")
                for h in range(2):
                    for cc in range(NCH):
                        ins = e.matmul(pf[qi % 2][:, 256 * h : 256 * (h + 1)],
                                       wqsrc[:, wqoff + 128 * cc : wqoff + 128 * cc + 128],
                                       xn[:, cc * 2048 + 512 * ns + 256 * h : cc * 2048 + 512 * ns + 256 * (h + 1)],
                                       start=(cc == 0), stop=(cc == 15))
                p.inc(ins, "tS")
                p.mark(f"batt_qmm_{j}_{ns}")

        def den_mm(j, ns_range):
            qt = qb[:, 2048 * (j % 2) : 2048 * (j % 2) + 2048]
            for ns in ns_range:
                mi = 4 * j + ns
                p.wait(e, f"batt_qev_{mi}", "aS")
                if mi >= 2:
                    p.wait(e, f"batt_ln_{mi-2}", "aS")
                dbank = pf[4 if ns % 2 == 0 else 6]
                for h in range(2):
                    ins = e.matmul(dbank[0:4, 256 * h : 256 * (h + 1)], sdenm[:, 4 * j : 4 * j + 4],
                                   qt[:, 512 * ns + 256 * h : 512 * ns + 256 * (h + 1)],
                                   start=True, stop=True)
                p.inc(ins, "tS")
                p.mark(f"batt_den_{mi}")

        def numbc(j):
            qt = qb[:, 2048 * (j % 2) : 2048 * (j % 2) + 2048]
            for ns in range(4):
                mi = 4 * j + ns
                if mi >= 2:
                    p.wait(e, f"batt_amix_{mi-2}", "vS")
                for h in range(2):
                    ins = e.matmul(pf[2 + mi % 2][:, 256 * h : 256 * (h + 1)],
                                   vkbd[:, 128 * j : 128 * (j + 1)],
                                   qt[:, 512 * ns + 256 * h : 512 * ns + 256 * (h + 1)],
                                   start=True, stop=True)
                p.inc(ins, "tS")
                p.mark(f"batt_num_{mi}")
                p.wait(e, f"batt_exp_{mi}", "aS")
                if mi == 0:
                    p.wait(e, "batt_rden0", "vS")
                if mi >= 2:
                    p.wait(e, f"batt_bce_{mi-2}", "aS")
                bbank = pf[5 if ns % 2 == 0 else 7]
                for h in range(2):
                    ins = e.matmul(bbank[:, 256 * h : 256 * (h + 1)], ind32[0:32, :],
                                   rden[0:32, 512 * (mi % 4) + 256 * h : 512 * (mi % 4) + 256 * (h + 1)],
                                   start=True, stop=True)
                p.inc(ins, "tS")
                p.mark(f"batt_bc_{mi}")

        q_mm(0)
        den_mm(0, (0, 1))
        den_mm(0, (2, 3))
        for j in range(1, NT):
            q_mm(j)
            den_mm(j, (0, 1))
            numbc(j - 1)
            den_mm(j, (2, 3))
        numbc(NT - 1)

        p.wait(e, f"batt_amix_{4*NT-1}", "vS")
        for cs in range(8):
            wt = wpT0 if cs % 2 == 0 else wpT1
            if cs >= 2:
                p.wait(e, f"batt_wptp_{cs}", "dT2")
            for n in range(16):
                pi = 16 * cs + n
                if pi >= 4:
                    p.wait(e, f"batt_o1_{pi-4}", "vS")
                for cc in range(NCH):
                    ins = e.matmul(pf[pi % 4][:, 0:256],
                                   big[:, cc * 2048 + 128 * n : cc * 2048 + 128 * n + 128],
                                   wt[:, 256 * cc : 256 * (cc + 1)], start=(cc == 0), stop=(cc == 15))
                p.inc(ins, "tS")
                p.mark(f"batt_pmm_{cs}_{n}")

    def Batt_scalar(e, p):
        def qev(j):
            for ns in range(4):
                qi = 4 * j + ns
                p.wait(e, f"batt_qmm_{j}_{ns}", "tS")
                p.inc(e.activation(qb[:, 2048 * (j % 2) + 512 * ns : 2048 * (j % 2) + 512 * (ns + 1)],
                                   pf[qi % 2][:, 0:512], ACT.Relu), "aS")
                p.mark(f"batt_qev_{qi}")

        def lnexp(j, ns_range):
            for ns in ns_range:
                mi = 4 * j + ns
                dbank = pf[4 if ns % 2 == 0 else 6]
                p.wait(e, f"batt_den_{mi}", "tS")
                if mi == 0:
                    p.wait(e, "batt_rden0", "vS")
                if mi >= 4:
                    p.wait(e, f"batt_bc_{mi-4}", "tS")
                p.inc(e.activation(rden[0:4, 512 * (mi % 4) : 512 * (mi % 4) + 512],
                                   dbank[0:4, 0:512],
                                   ACT.Ln, bias=scst[0:4, C_1E15 : C_1E15 + 1]), "aS")
                p.mark(f"batt_ln_{mi}")
                if p.emit:
                    e.drain()
                e.activation(rden[0:4, 512 * (mi % 4) : 512 * (mi % 4) + 512],
                             rden[0:4, 512 * (mi % 4) : 512 * (mi % 4) + 512],
                             ACT.Exp, scale=scst[0:4, C_NEG1 : C_NEG1 + 1])
                p.adrain(e)
                p.mark(f"batt_exp_{mi}")

        def bce(j):
            for ns in range(4):
                mi = 4 * j + ns
                bbank = pf[5 if ns % 2 == 0 else 7]
                p.wait(e, f"batt_bc_{mi}", "tS")
                p.inc(e.activation(bcsb[:, 512 * (mi % 2) : 512 * (mi % 2) + 512], bbank[:, 0:512],
                                   ACT.Copy), "aS")
                p.mark(f"batt_bce_{mi}")

        qev(0)
        lnexp(0, (0, 1))
        lnexp(0, (2, 3))
        for j in range(1, NT):
            qev(j)
            lnexp(j, (0, 1))
            bce(j - 1)
            lnexp(j, (2, 3))
        bce(NT - 1)

    def Batt_vector(e, p):
        p.inc(e.memset(rden[:, :], 0.0), "vS")
        p.mark("batt_rden0")

        def amix(j):
            for ns in range(4):
                mi = 4 * j + ns
                p.wait(e, f"batt_num_{mi}", "tS")
                p.wait(e, f"batt_bce_{mi}", "aS")
                p.inc(e.tensor_tensor(big[:, j * 2048 + 512 * ns : j * 2048 + 512 * (ns + 1)],
                                      pf[2 + mi % 2][:, 0:512],
                                      bcsb[:, 512 * (mi % 2) : 512 * (mi % 2) + 512], ALU.mult), "vS")
                p.mark(f"batt_amix_{mi}")

        for j in range(NT):
            amix(j)
        p.wait(e, "batt_bp", "dW3")
        for cs in range(8):
            for n in range(16):
                pi = 16 * cs + n
                p.wait(e, f"batt_pmm_{cs}_{n}", "tS")
                if pi >= 2:
                    p.wait(e, f"batt_o1w_{pi-2}", f"dX{2 + pi % 2}")
                p.inc(e.tensor_tensor(oev[:, 256 * (pi % 4) : 256 * (pi % 4) + 256],
                                      pf[pi % 4][:, 0:256],
                                      sbpjb[:, 256 * cs : 256 * (cs + 1)], ALU.add), "vS")
                p.mark(f"batt_o1_{pi}")

    def C1_gp(e, p):
        for ee in range(NE):
            if ee >= 2:
                p.wait(e, f"c1_mm_{ee-2}_3_15", "tS")
            src = ebt_h[ee, :, :]
            dst = ebb[:, 2048 * (ee % 2) : 2048 * (ee % 2) + 2048]
            p.inc(e.dma_start(out=dst, in_=src), f"dW{ee % 2}", 16)
            p.mark(f"c1_eb_{ee}")
            for fs in range(4):
                qi = 4 * ee + fs
                if qi >= 2:
                    pe_, pfs = divmod(qi - 2, 4)
                    p.wait(e, f"c1_mm_{pe_}_{pfs}_15", "tS")
                src = ew_h[ee, :, 512 * fs : 512 * (fs + 1)].rearrange("(j p) f -> p j f", p=128)
                dst = eq[:, 8192 * (qi % 2) : 8192 * (qi % 2 + 1)].rearrange("p (j f) -> p j f", j=16)
                p.inc(e.dma_start(out=dst, in_=src), f"dW{2 + qi % 2}", 16)
                p.mark(f"c1_q_{ee}_{fs}")

    def C1_tensor(e, p):
        for ee in range(NE):
            p.wait(e, f"c1_eb_{ee}", f"dW{ee % 2}")
            for fs in range(4):
                qi = 4 * ee + fs
                p.wait(e, f"c1_q_{ee}_{fs}", f"dW{2 + qi % 2}")
                buf = eq[:, 8192 * (qi % 2) : 8192 * (qi % 2 + 1)]
                for t in range(NT):
                    gi = 16 * qi + t
                    if gi >= 6:
                        p.wait(e, f"c1_hb_{gi-6}", "vS")
                    for h in range(2):
                        for dc in range(NCH):
                            ins = e.matmul(pf[gi % 6][:, 256 * h : 256 * (h + 1)],
                                           xn[:, dc * 2048 + 128 * t : dc * 2048 + 128 * t + 128],
                                           buf[:, 512 * dc + 256 * h : 512 * dc + 256 * (h + 1)],
                                           start=(dc == 0), stop=(dc == 15))
                    p.inc(ins, "tS")
                    p.mark(f"c1_mm_{ee}_{fs}_{t}")

    def C1_vector(e, p):
        def mac(gi):
            qi2, t2 = divmod(gi, NT)
            ee2, fs2 = divmod(qi2, 4)
            p.wait(e, f"c1_silu_{gi}", "aS")
            dst = big[:, t2 * 2048 + 512 * fs2 : t2 * 2048 + 512 * (fs2 + 1)]
            sl = silu[:, 512 * (gi % 2) : 512 * (gi % 2) + 512]
            fwc = sfw[:, 8 * t2 + ee2 : 8 * t2 + ee2 + 1]
            if ee2 == 0:
                p.inc(e.tensor_scalar(dst, sl, fwc, scst[:, C_ZERO : C_ZERO + 1],
                                      ALU.mult, ALU.add), "vS")
            else:
                p.inc(e.scalar_tensor_tensor(dst, sl, fwc, dst, ALU.mult, ALU.add), "vS")
            p.mark(f"c1_mac_{gi}")

        for gi in range(NE * 4 * NT):
            qi, t = divmod(gi, NT)
            ee, fs = divmod(qi, 4)
            p.wait(e, f"c1_mm_{ee}_{fs}_{t}", "tS")
            if gi >= 2:
                p.wait(e, f"c1_silu_{gi-2}", "aS")
            p.inc(e.tensor_tensor(hbuf[:, 512 * (gi % 2) : 512 * (gi % 2) + 512],
                                  pf[gi % 6][:, 0:512],
                                  ebb[:, 2048 * (ee % 2) + 512 * fs : 2048 * (ee % 2) + 512 * (fs + 1)],
                                  ALU.add), "vS")
            p.mark(f"c1_hb_{gi}")
            if gi >= 1:
                mac(gi - 1)
        mac(NE * 4 * NT - 1)

    def C1_scalar(e, p):
        for gi in range(NE * 4 * NT):
            p.wait(e, f"c1_hb_{gi}", "vS")
            if gi >= 2:
                p.wait(e, f"c1_mac_{gi-2}", "vS")
            p.inc(e.activation(silu[:, 512 * (gi % 2) : 512 * (gi % 2) + 512],
                               hbuf[:, 512 * (gi % 2) : 512 * (gi % 2) + 512], ACT.Silu), "aS")
            p.mark(f"c1_silu_{gi}")

    def C2_gp(e, p):
        p.inc(e.dma_start(out=cpad_s[:, :], in_=cpad_h[:, :]), "dW4", 16)
        p.mark("c2_cpad")
        p.wait(e, "c1_mm_7_2_15", "tS")
        p.inc(e.dma_start(out=pbb[:, :], in_=pbt_h[:, :]), "dW0", 16)
        p.inc(e.dma_start(out=snwb[:, :], in_=nwt_h[:, :]), "dW0", 16)
        p.mark("c2_nw")
        for s in range(4):
            p.wait(e, f"c2_tp_{4*s+3}", f"dT{s}")
            src = pw_h[:, 512 * s : 512 * (s + 1)].rearrange("(j p) f -> p j f", p=128)
            dst = pwq[:, 8192 * s : 8192 * (s + 1)].rearrange("p (j f) -> p j f", j=16)
            p.inc(e.dma_start(out=dst, in_=src), "dW1", 16)
            p.mark(f"c2_pw_{s}")

    def C1_sync(e, p):
        for tj in range(NT):
            p.wait(e, f"c1_mac_{496 + tj}", "vS")
            p.wait(e, f"c1_mm_7_3_{tj}", "tS")
            src = big[:, 2048 * tj : 2048 * (tj + 1)]
            dst = combT[:, :].rearrange("p (fc t) -> p fc t", t=2048)[:, :, 128 * tj : 128 * tj + 128]
            p.inc(e.dma_start_transpose(out=dst, in_=src), f"dT{tj // 4}", 16)
            p.mark(f"c2_tp_{tj}")

    def C2_sync(e, p):
        for t in range(NT):
            if t >= 1:
                p.wait(e, f"c2_w_{t-1}", "dX2")
                p.wait(e, f"c2_w_{t-1}", "dX3")
            p.inc(e.dma_start(out=o1rd[:, 0:1024], in_=out_h[128 * t : 128 * (t + 1), 0:1024]),
                  "dX1", 16)
            p.inc(e.dma_start(out=o1rd[:, 1024:2048], in_=out_h[128 * t : 128 * (t + 1), 1024:2048]),
                  "dX2", 16)
            p.mark(f"c2_o1_{t}")
            if t >= 1:
                p.wait(e, f"c2_o1x_{t-1}", "vS")
            p.inc(e.dma_start(out=xst[:, 0:2048], in_=x_h[128 * t : 128 * (t + 1), :]), "dX0", 16)
            p.mark(f"c2_x_{t}")
            p.wait(e, f"c2_fin_{t}", "vS")
            p.inc(e.dma_start(out=out_h[128 * t : 128 * (t + 1), 0:1024], in_=o1rd[:, 0:1024]),
                  "dX2", 16)
            p.inc(e.dma_start(out=out_h[128 * t : 128 * (t + 1), 1024:2048], in_=o1rd[:, 1024:2048]),
                  "dX3", 16)
            p.mark(f"c2_w_{t}")
        if p.emit:
            for s_ in DMA_SEMS:
                if p.cnt[s_] > 0:
                    e.wait_ge(p.sems[s_], p.cnt[s_])

    def C2_tensor(e, p):
        p.wait(e, "c2_pw_3", "dW1")
        p.wait(e, "c2_cpad", "dW4")
        p.wait(e, "c2_nw", "dW0")
        for t in range(NT):
            p.wait(e, f"c2_tp_{4 * (t // 4) + 3}", f"dT{t // 4}")
            if t >= 2:
                p.wait(e, f"c2_yadd_{t-2}", "vS")
            pq = [pf[4 * (t % 2) + s] for s in range(4)]
            for h in range(2):
                for fc in range(NCH):
                    for s in range(4):
                        e.matmul(pq[s][:, 256 * h : 256 * (h + 1)],
                                 combT[:, fc * 2048 + 128 * t : fc * 2048 + 128 * t + 128],
                                 pwq[:, 8192 * s + 512 * fc + 256 * h : 8192 * s + 512 * fc + 256 * (h + 1)],
                                 start=(fc == 0), stop=False)
                for s in range(4):
                    ins = e.matmul(pq[s][:, 256 * h : 256 * (h + 1)], cpad_s[:, :],
                                   pbb[:, 512 * s + 256 * h : 512 * s + 256 * h + 256],
                                   start=False, stop=True)
            p.inc(ins, "tS")
            p.mark(f"c2_mm_{t}")

    def C2_vector(e, p):
        p.wait(e, "c2_nw", "dW0")
        for t in range(NT):
            p.wait(e, f"c2_mm_{t}", "tS")
            p.wait(e, f"c2_x_{t}", "dX0")
            for s in range(4):
                p.D(e, e.tensor_scalar(ybuf[:, 512 * s : 512 * (s + 1)], xst[:, 512 * s : 512 * (s + 1)],
                                       srstd[:, t : t + 1], snegmr[:, t : t + 1], ALU.mult, ALU.add))
                tt = e.tensor_tensor(ybuf[:, 512 * s : 512 * (s + 1)], ybuf[:, 512 * s : 512 * (s + 1)],
                                     pf[4 * (t % 2) + s][:, 0:512], ALU.add)
                if s < 3:
                    p.D(e, tt)
                else:
                    p.inc(tt, "vS")
            p.mark(f"c2_yadd_{t}")
            p.mark(f"c2_y_{t}")
            p.wait(e, f"c2_o1_{t}", "dX1")
            p.wait(e, f"c2_o1_{t}", "dX2")
            p.inc(e.tensor_tensor(o1rd[:, :], o1rd[:, :], xst[:, 0:2048], ALU.add), "vS")
            p.mark(f"c2_o1x_{t}")
            p.wait(e, f"c2_rms_{t}", "aS")
            p.D(e, e.scalar_tensor_tensor(ybuf[:, :], ybuf[:, :], stats[:, 2:3], snwb[:, :],
                                          ALU.mult, ALU.mult))
            p.inc(e.tensor_tensor(o1rd[:, :], ybuf[:, :], o1rd[:, :], ALU.add), "vS")
            p.mark(f"c2_fin_{t}")

    def C2_scalar(e, p):
        for t in range(NT):
            p.wait(e, f"c2_y_{t}", "vS")
            p.inc(e.activation(sdump2[:, :], ybuf[:, :], ACT.Square, accum_out=stats[:, 1:2]), "aS")
            p.mark(f"c2_ssq_{t}")
            if p.emit:
                e.drain()
            e.activation(stats[:, 3:4], stats[:, 1:2], ACT.Ln, scale=scst[:, C_I2048 : C_I2048 + 1],
                         bias=scst[:, C_EPSRMS : C_EPSRMS + 1])
            if p.emit:
                e.drain()
            e.activation(stats[:, 2:3], stats[:, 3:4], ACT.Exp, scale=scst[:, C_NEG05 : C_NEG05 + 1])
            p.adrain(e)
            p.mark(f"c2_rms_{t}")

    phases = []

    def add(engine, fn):
        phases.append((engine, fn))

    def add_barrier(tag, light=()):
        def end(sem):
            def f(e, p):
                if p.emit:
                    e.drain()
                p.inc(e.nop(), sem)
                p.mark(f"bar_{tag}_{sem}")
            return f

        def wt(skip_dma):
            def f(e, p):
                for s in ("tS", "vS", "aS"):
                    p.wait(e, f"bar_{tag}_{s}", s)
                if p.emit and not skip_dma:
                    for s in DMA_SEMS:
                        v = p.val(f"bar_{tag}_tS", s)
                        if v > 0:
                            e.wait_ge(p.sems[s], v)
            return f

        add("vector", end("vS"))
        add("scalar", end("aS"))
        add("tensor", end("tS"))
        for eng in ("tensor", "vector", "scalar", "gpsimd", "sync"):
            add(eng, wt(eng in light))

    add("sync", setup_sync)
    add("gpsimd", setup_gp)
    add("vector", setup_vec)
    add_barrier("s")
    if upto >= 2:
        add("sync", A_sync)
        add("scalar", A_scalar)
        add("vector", A_vector)
        add("tensor", A_tensor)
        add_barrier("a")
    if upto >= 3:
        add("gpsimd", Bkv_gp)
        add("sync", Bkv_sync)
        add("tensor", Bkv_tensor)
        add("scalar", Bkv_scalar)
        add("vector", Bkv_vector)
        add_barrier("bk")
    if upto >= 4:
        add("gpsimd", Batt_gp)
        add("sync", Batt_sync)
        add("tensor", Batt_tensor)
        add("scalar", Batt_scalar)
        add("vector", Batt_vector)
        add_barrier("ba")
    if upto >= 5:
        add("gpsimd", C1_gp)
        add("tensor", C1_tensor)
        add("scalar", C1_scalar)
        add("vector", C1_vector)
        if upto >= 6:
            add("sync", C1_sync)
            add("gpsimd", C2_gp)
        add_barrier("c1", light=("tensor", "vector", "scalar", "sync") if upto >= 6 else ())
    if upto >= 6:
        add("sync", C2_sync)
        add("tensor", C2_tensor)
        add("vector", C2_vector)
        add("scalar", C2_scalar)

    if upto < 6:
        def finisher(e, p):
            if p.emit:
                for s_ in DMA_SEMS:
                    if p.cnt[s_] > 0:
                        e.wait_ge(p.sems[s_], p.cnt[s_])
        add("sync", finisher)

    p = Ctx()
    mock = MockEng()
    for eng, fn in phases:
        fn(mock, p)
    totals = dict(p.cnt)

    p.emit = True
    p.sems = sems
    for s in SEMS:
        p.cnt[s] = 0
    with nc.Block() as block:
        for eng_name, fn in phases:
            getattr(block, eng_name)(lambda e, fn=fn: fn(e, p))
    assert p.cnt == totals, (p.cnt, totals)

    ctx.close()
    return nc


def make_in_maps(L0_attn_x, L0_attn_w_qkv, L0_attn_w_proj, L0_attn_b_proj,
                 L0_ffn_router_w, L0_ffn_router_b, L0_ffn_expert_w, L0_ffn_expert_b,
                 L0_ffn_proj_w, L0_ffn_proj_b, L0_ffn_norm_w):
    ind32_np = np.zeros((32, 128), np.float32)
    for i in range(4):
        ind32_np[i, 32 * i : 32 * (i + 1)] = 1.0
    cpad_np = np.zeros((128, 128), np.float32)
    cpad_np[0, :] = 1.0
    pbrow = np.zeros((128, C), np.float32)
    pbrow[0, :] = np.asarray(L0_ffn_proj_b, dtype=np.float32)

    def tile(a):
        a = np.asarray(a, dtype=np.float32).reshape(1, -1)
        return np.ascontiguousarray(np.broadcast_to(a, (128, a.shape[1])))

    common = {
        "wqkv": np.ascontiguousarray(L0_attn_w_qkv, dtype=np.float32),
        "wproj": np.ascontiguousarray(L0_attn_w_proj, dtype=np.float32),
        "bpt": tile(L0_attn_b_proj),
        "rw": np.ascontiguousarray(L0_ffn_router_w, dtype=np.float32),
        "rbt": tile(L0_ffn_router_b),
        "ew": np.ascontiguousarray(L0_ffn_expert_w, dtype=np.float32),
        "ebt": np.ascontiguousarray(
            np.broadcast_to(np.asarray(L0_ffn_expert_b, dtype=np.float32)[:, None, :],
                            (NE, 128, C))),
        "pw": np.ascontiguousarray(L0_ffn_proj_w, dtype=np.float32),
        "pbt": pbrow,
        "nwt": tile(L0_ffn_norm_w),
        "ind32": ind32_np,
        "cpad": cpad_np,
    }
    in_maps = []
    for b in range(8):
        m = dict(common)
        m["x"] = np.ascontiguousarray(L0_attn_x[b], dtype=np.float32)
        in_maps.append(m)
    return in_maps


def kernel(L0_attn_x, L0_attn_w_qkv, L0_attn_w_proj, L0_attn_b_proj,
           L0_ffn_router_w, L0_ffn_router_b, L0_ffn_expert_w, L0_ffn_expert_b,
           L0_ffn_proj_w, L0_ffn_proj_b, L0_ffn_norm_w):
    nc = build_program()
    in_maps = make_in_maps(L0_attn_x, L0_attn_w_qkv, L0_attn_w_proj, L0_attn_b_proj,
                           L0_ffn_router_w, L0_ffn_router_b, L0_ffn_expert_w, L0_ffn_expert_b,
                           L0_ffn_proj_w, L0_ffn_proj_b, L0_ffn_norm_w)
    res = run_bass_kernel_spmd(nc, in_maps, core_ids=list(range(8)))
    return np.stack([np.asarray(r["out"]) for r in res.results], axis=0).astype(np.float32)


# version 14
# speedup vs baseline: 1.2369x; 1.0505x over previous
import numpy as np
import ml_dtypes
import concourse.bass as bass
import concourse.mybir as mybir
from concourse.bass_utils import run_bass_kernel_spmd

F32 = mybir.dt.float32
BF16 = mybir.dt.bfloat16
AX = mybir.AxisListType.X
ALU = mybir.AluOpType
ACT = mybir.ActivationFunctionType

N = 2048
C = 2048
NT = 16
NCH = 16
NE = 8

SEMS = ("tS", "vS", "aS", "vD", "gS",
        "dX0", "dX1", "dX2", "dX3", "dW0", "dW1", "dW2", "dW3", "dW4",
        "dT0", "dT1", "dT2", "dT3", "dT4")
DMA_SEMS = SEMS[5:]


class Ctx:
    def __init__(self):
        self.cnt = {s: 0 for s in SEMS}
        self.ev = {}
        self.emit = False
        self.sems = None

    def inc(self, instr, sem, k=1):
        self.cnt[sem] += k
        if self.emit:
            instr.then_inc(self.sems[sem], k)
        return self.cnt[sem]

    def mark(self, key):
        if not self.emit:
            assert key not in self.ev, key
            self.ev[key] = dict(self.cnt)

    def val(self, key, sem):
        assert key in self.ev, f"missing event {key}"
        return self.ev[key][sem]

    def wait(self, eng, key, sem):
        if self.emit:
            v = self.val(key, sem)
            if v > 0:
                eng.wait_ge(self.sems[sem], v)

    def D(self, v, instr):
        self.cnt["vD"] += 1
        if self.emit:
            instr.then_inc(self.sems["vD"], 1)
            v.wait_ge(self.sems["vD"], self.cnt["vD"])

    def adrain(self, s):
        self.cnt["aS"] += 1
        if self.emit:
            s.drain()
            s.nop().then_inc(self.sems["aS"], 1)


class MockInstr:
    def then_inc(self, *a, **k):
        return self


class MockEng:
    def __getattr__(self, name):
        def f(*a, **k):
            return MockInstr()
        return f


def build_program(upto=6):
    nc = bass.Bass("TRN2", target_bir_lowering=False)

    x_h = nc.dram_tensor("x", [N, C], F32, kind="ExternalInput").ap()
    wqkv_h = nc.dram_tensor("wqkv", [3 * C, C], F32, kind="ExternalInput").ap()
    wproj_h = nc.dram_tensor("wproj", [C, C], F32, kind="ExternalInput").ap()
    bpt_h = nc.dram_tensor("bpt", [128, C], F32, kind="ExternalInput").ap()
    rw_h = nc.dram_tensor("rw", [C, 8], F32, kind="ExternalInput").ap()
    rbt_h = nc.dram_tensor("rbt", [128, 8], F32, kind="ExternalInput").ap()
    ew_h = nc.dram_tensor("ew", [NE, C, C], F32, kind="ExternalInput").ap()
    ebt_h = nc.dram_tensor("ebt", [NE, 128, C], F32, kind="ExternalInput").ap()
    pw_h = nc.dram_tensor("pw", [C, C], F32, kind="ExternalInput").ap()
    pbt_h = nc.dram_tensor("pbt", [128, C], F32, kind="ExternalInput").ap()
    nwt_h = nc.dram_tensor("nwt", [128, C], F32, kind="ExternalInput").ap()
    i32_h = nc.dram_tensor("ind32", [32, 128], F32, kind="ExternalInput").ap()
    cpad_h = nc.dram_tensor("cpad", [128, 128], F32, kind="ExternalInput").ap()
    out_h = nc.dram_tensor("out", [N, C], F32, kind="ExternalOutput").ap()

    import contextlib
    ctx = contextlib.ExitStack()
    RB = ctx.enter_context(nc.sbuf_tensor("RB", [128, 92256], BF16))
    RF = ctx.enter_context(nc.sbuf_tensor("RF", [128, 6656], F32))
    RBa = RB[:, :]
    RFa = RF[:, :]

    W0 = 65536
    xn = RBa[:, 0:32768]
    combT = RBa[:, 0:32768]
    big = RBa[:, 32768:65536]
    pwq = RBa[:, 32768:65536]
    wk_t = RBa[:, W0 + 0 : W0 + 4096]
    wv_t = RBa[:, W0 + 4096 : W0 + 8192]
    k_s = RBa[:, W0 + 8192 : W0 + 12288]
    v_s = RBa[:, W0 + 12288 : W0 + 16384]
    wpT0 = RBa[:, W0 + 0 : W0 + 4096]
    wpT1 = RBa[:, W0 + 4096 : W0 + 8192]
    wstg = RBa[:, W0 + 8192 : W0 + 12288]
    wqnat = RBa[:, W0 + 12288 : W0 + 16384]
    wqT = RBa[:, W0 + 16384 : W0 + 20480]
    qb = RBa[:, W0 + 20480 : W0 + 24576]
    eq = RBa[:, W0 + 0 : W0 + 16384]
    ebb = RBa[:, W0 + 16384 : W0 + 20480]
    sdump = RBa[:, W0 + 0 : W0 + 2048]
    srw = RBa[:, W0 + 2048 : W0 + 2176]
    sdump2 = RBa[:, W0 + 4096 : W0 + 6144]
    pbb = RBa[:, W0 + 0 : W0 + 2048]
    snwb = RBa[:, W0 + 2048 : W0 + 4096]
    cpad_s = RBa[:, W0 + 20480 : W0 + 20608]
    ones32 = RBa[:, 90112:90144]
    sdenm = RBa[:, 90144:90208]
    vkbd = RBa[:, 90208:92256]
    xst = RFa[:, 0:4096]
    xsl = RFa[:, 0:1024]
    oev = RFa[:, 4096:5120]
    rden = RFa[:, 1024:3072]
    bcsb = RFa[:, 3072:4096]
    sbpjb = RFa[:, 1024:3072]
    silu = RFa[:, 0:1024]
    lgall = RFa[:, 0:128]
    hbuf = RFa[:, 1024:2048]
    o1rd = RFa[:, 2048:4096]
    ybuf = RFa[:, 4096:6144]
    smean = RFa[:, 6144:6160]
    srstd = RFa[:, 6160:6176]
    snegmr = RFa[:, 6176:6192]
    sfw = RFa[:, 6192:6320]
    scst = RFa[:, 6320:6336]
    srbb = RFa[:, 6336:6344]
    stats = RFa[:, 6344:6424]
    ind32 = RFa[0:32, 6424:6552]

    C_ZERO, C_ONE, C_N1E30, C_EPSLN, C_EPSRMS, C_1E15, C_NEG1, C_NEG05, C_I2048 = range(9)

    pf = [ctx.enter_context(nc.psum_tensor(f"pf{i}", [128, 512], F32)) for i in range(8)]

    sems = {s: ctx.enter_context(nc.semaphore(s)) for s in SEMS}


    def setup_sync(e, p):
        p.inc(e.dma_start(out=ind32[:, :], in_=i32_h[:, :]), "dX0", 16)
        p.inc(e.dma_start(out=srbb[:, :], in_=rbt_h[:, :]), "dX0", 16)

    def setup_gp(e, p):
        for j in range(16):
            p.inc(e.dma_start(out=srw[:, 8 * j : 8 * j + 8],
                              in_=rw_h[128 * j : 128 * (j + 1), :]), "dW0", 16)

    def setup_vec(e, p):
        vals = [0.0, 1.0, -1e30, 1e-5, 1e-6, 1e-15, -1.0, -0.5, 1.0 / 2048.0]
        for i, val in enumerate(vals):
            e.memset(scst[:, i : i + 1], val)
        e.memset(ones32[:, :], 1.0)
        e.memset(vkbd[:, :], 0.0)
        p.D(e, e.memset(sdenm[:, :], 0.0))

    def A_sync(e, p):
        for j in range(2):
            p.inc(e.dma_start(out=xst[:, 2048 * j : 2048 * j + 2048],
                              in_=x_h[128 * j : 128 * (j + 1), :]), f"dX{j % 2}", 16)
            p.mark(f"a_x_{j}")
        for j in range(NT):
            p.wait(e, f"a_xn_{j}", "vS")
            src = wqnat[:, 2048 * (j % 2) : 2048 * (j % 2) + 2048]
            dst = xn[:, :].rearrange("p (cc t) -> p cc t", t=2048)[:, :, 128 * j : 128 * j + 128]
            p.inc(e.dma_start_transpose(out=dst, in_=src), f"dT{j % 2}", 16)
            p.mark(f"a_tp_{j}")
            if j + 2 < NT:
                p.inc(e.dma_start(out=xst[:, 2048 * ((j + 2) % 2) : 2048 * ((j + 2) % 2) + 2048],
                                  in_=x_h[128 * (j + 2) : 128 * (j + 3), :]), f"dX{j % 2}", 16)
                p.mark(f"a_x_{j+2}")

    def A_scalar(e, p):
        for j in range(NT):
            xt = xst[:, 2048 * (j % 2) : 2048 * (j % 2) + 2048]
            p.wait(e, f"a_x_{j}", f"dX{j % 2}")
            p.inc(e.activation(sdump[:, :], xt, ACT.Square, accum_out=stats[:, 0:1]), "aS")
            p.mark(f"a_ssq_{j}")
            p.wait(e, f"a_var_{j}", "vS")
            e.activation(stats[:, 4:5], stats[:, 3:4], ACT.Ln, bias=scst[:, C_EPSLN : C_EPSLN + 1])
            if p.emit:
                e.drain()
            e.activation(srstd[:, j : j + 1], stats[:, 4:5], ACT.Exp,
                         scale=scst[:, C_NEG05 : C_NEG05 + 1])
            p.adrain(e)
            p.mark(f"a_rstd_{j}")

    def A_vector(e, p):
        def wchain(j):
            RB = 8 + 36 * (j % 2)
            p.wait(e, f"a_exp_{j}", "aS")
            p.D(e, e.tensor_scalar(stats[:, RB + 28 : RB + 29], stats[:, RB + 27 : RB + 28],
                                   scst[:, C_ONE : C_ONE + 1],
                                   scst[:, C_ONE : C_ONE + 1], ALU.mult, ALU.add))
            p.D(e, e.reciprocal(stats[:, RB + 29 : RB + 30], stats[:, RB + 28 : RB + 29]))
            p.D(e, e.tensor_tensor(stats[:, RB + 30 : RB + 31], stats[:, RB + 27 : RB + 28],
                                   stats[:, RB + 29 : RB + 30], ALU.mult))
            p.D(e, e.tensor_scalar(sfw[:, 8 * j : 8 * j + 8], stats[:, RB + 8 : RB + 16],
                                   stats[:, RB + 29 : RB + 30],
                                   scst[:, C_ZERO : C_ZERO + 1], ALU.mult, ALU.add))
            p.inc(e.scalar_tensor_tensor(sfw[:, 8 * j : 8 * j + 8], stats[:, RB + 16 : RB + 24],
                                          stats[:, RB + 30 : RB + 31],
                                          sfw[:, 8 * j : 8 * j + 8], ALU.mult, ALU.add), "vS")
            p.mark(f"a_top2_{j}")

        for j in range(NT):
            xt = xst[:, 2048 * (j % 2) : 2048 * (j % 2) + 2048]
            p.wait(e, f"a_x_{j}", f"dX{j % 2}")
            p.D(e, e.reduce_sum(stats[:, 1:2], xt, axis=AX))
            p.wait(e, f"a_ssq_{j}", "aS")
            p.D(e, e.tensor_scalar(smean[:, j : j + 1], stats[:, 1:2], scst[:, C_I2048 : C_I2048 + 1],
                                   scst[:, C_ZERO : C_ZERO + 1], ALU.mult, ALU.add))
            p.D(e, e.tensor_scalar(stats[:, 2:3], stats[:, 0:1], scst[:, C_I2048 : C_I2048 + 1],
                                   scst[:, C_ZERO : C_ZERO + 1], ALU.mult, ALU.add))
            p.D(e, e.tensor_tensor(stats[:, 5:6], smean[:, j : j + 1], smean[:, j : j + 1], ALU.mult))
            p.inc(e.tensor_tensor(stats[:, 3:4], stats[:, 2:3], stats[:, 5:6], ALU.subtract), "vS")
            p.mark(f"a_var_{j}")
            p.wait(e, f"a_rstd_{j}", "aS")
            p.D(e, e.tensor_tensor(stats[:, 6:7], smean[:, j : j + 1], srstd[:, j : j + 1], ALU.mult))
            p.D(e, e.tensor_scalar(snegmr[:, j : j + 1], stats[:, 6:7], -1.0, None, ALU.mult))
            xbf = wqnat[:, 2048 * (j % 2) : 2048 * (j % 2) + 2048]
            if j >= 2:
                p.wait(e, f"a_tp_{j-2}", f"dT{j % 2}")
            p.inc(e.tensor_scalar(xbf, xt, srstd[:, j : j + 1], snegmr[:, j : j + 1],
                                  ALU.mult, ALU.add), "vS")
            p.mark(f"a_xn_{j}")

    def A_tensor(e, p):
        for j in range(NT):
            p.wait(e, f"a_tp_{j}", f"dT{j % 2}")
            for cc in range(NCH):
                ins = e.matmul(pf[0][:, 8 * j : 8 * j + 8],
                               xn[:, cc * 2048 + 128 * j : cc * 2048 + 128 * j + 128],
                               srw[:, 8 * cc : 8 * cc + 8], start=(cc == 0), stop=(cc == 15))
            p.inc(ins, "tS")
            p.mark(f"a_rt_{j}")

    def Bkv_gp(e, p):
        for s in range(8):
            for kv in range(2):
                slot = (2 * s + kv) % 4
                if 2 * s + kv >= 4:
                    pidx = 2 * s + kv - 4
                    ps_, pkv = pidx // 2, pidx % 2
                    p.wait(e, f"bkv_tp_{ps_}_{pkv}", f"dT{2 * pkv + ps_ % 2}")
                src = wqkv_h[C * (1 + kv) + 256 * s : C * (1 + kv) + 256 * (s + 1), :]
                src = src.rearrange("(oc p) c -> p oc c", p=128)
                dst = big[:, 4096 * slot : 4096 * (slot + 1)].rearrange("p (oc c) -> p oc c", oc=2)
                p.inc(e.dma_start(out=dst, in_=src), f"dW{slot}", 16)
                p.mark(f"bkv_w_{s}_{kv}")
        for j in range(2):
            p.inc(e.dma_start(out=big[:, 20480 + 2048 * j : 20480 + 2048 * j + 2048],
                              in_=wqkv_h[128 * j : 128 * (j + 1), :]), "dW4", 16)
            p.mark(f"batt_wq_{j}")
        for cs in range(2):
            for g in range(4):
                p.wait(e, "bkv_tp_7_1", f"dT{g}")
            src = wproj_h[256 * cs : 256 * (cs + 1), :].rearrange("(oc p) c -> p oc c", p=128)
            dst = big[:, 4096 * cs : 4096 * (cs + 1)].rearrange("p (oc c) -> p oc c", oc=2)
            p.inc(e.dma_start(out=dst, in_=src), f"dW{cs}", 16)
            p.mark(f"bkv_wp_{cs}")

    def Bkv_sync(e, p):
        for s in range(8):
            for kv in range(2):
                slot = (2 * s + kv) % 4
                stage = big[:, 4096 * slot : 4096 * (slot + 1)]
                if kv == 0:
                    wt = wk_t if s % 2 == 0 else wqT
                else:
                    wt = wv_t if s % 2 == 0 else qb
                p.wait(e, f"bkv_w_{s}_{kv}", f"dW{slot}")
                if s >= 2:
                    p.wait(e, f"bkv_kvmm_{s-2}", "tS")
                for oc in range(2):
                    src = stage[:, 2048 * oc : 2048 * (oc + 1)]
                    dst = wt[:, :].rearrange("p (cc o) -> p cc o", o=256)[:, :, 128 * oc : 128 * oc + 128]
                    p.inc(e.dma_start_transpose(out=dst, in_=src), f"dT{2 * kv + s % 2}", 16)
                p.mark(f"bkv_tp_{s}_{kv}")
        for j in range(2):
            p.wait(e, "batt_wq_1", "dW4")
            src = big[:, 20480 + 2048 * j : 20480 + 2048 * j + 2048]
            dst = big[:, 16384 + 2048 * j : 16384 + 2048 * j + 2048].rearrange(
                "p (cc m) -> p cc m", m=128)
            p.inc(e.dma_start_transpose(out=dst, in_=src), "dT4", 16)
            p.mark(f"batt_tp_{j}")
        for cs in range(2):
            p.wait(e, f"bkv_wp_{cs}", f"dW{cs}")
            p.wait(e, "bkv_kvmm_7", "tS")
            stage = big[:, 4096 * cs : 4096 * (cs + 1)]
            wt = wpT0 if cs == 0 else wpT1
            for oc in range(2):
                src = stage[:, 2048 * oc : 2048 * (oc + 1)]
                dst = wt[:, :].rearrange("p (cc o) -> p cc o", o=256)[:, :, 128 * oc : 128 * oc + 128]
                p.inc(e.dma_start_transpose(out=dst, in_=src), "dT2", 16)
            p.mark(f"bkv_wptp_{cs}")

    def Bkv_tensor(e, p):
        p.wait(e, "lg_copied", "vS")
        for s in range(8):
            p.wait(e, f"bkv_tp_{s}_1", f"dT{s % 2}")
            p.wait(e, f"bkv_tp_{s}_1", f"dT{2 + s % 2}")
            for n in range(16):
                for kv in range(2):
                    gk = 16 * s + n
                    bank = pf[kv * 2 + n % 2]
                    if gk >= 2:
                        ps_, pn = divmod(gk - 2, 16)
                        if kv == 0:
                            p.wait(e, f"bkv_kev_{ps_}_{pn}", "aS")
                        else:
                            p.wait(e, f"bkv_vev_{ps_}_{pn}", "vS")
                    if kv == 0:
                        wt = wk_t if s % 2 == 0 else wqT
                    else:
                        wt = wv_t if s % 2 == 0 else qb
                    for cc in range(NCH):
                        ins = e.matmul(bank[:, 0:256],
                                       xn[:, cc * 2048 + 128 * n : cc * 2048 + 128 * n + 128],
                                       wt[:, 256 * cc : 256 * (cc + 1)], start=(cc == 0), stop=(cc == 15))
                    p.inc(ins, "tS")
                    p.mark(f"bkv_{'k' if kv == 0 else 'v'}mm_{s}_{n}")
            p.mark(f"bkv_kvmm_{s}")
            p.wait(e, f"bkv_kev_{s}_15", "aS")
            p.wait(e, f"bkv_vev_{s}_15", "vS")
            if s > 0:
                p.wait(e, f"bkv_vke_{s-1}", "vS")
            for g in range(2):
                jg = 2 * s + g
                for n in range(16):
                    e.matmul(pf[4][:, 128 * g : 128 * g + 128],
                             k_s[:, 256 * n + 128 * g : 256 * n + 128 * g + 128],
                             v_s[:, 256 * n + 128 * g : 256 * n + 128 * g + 128],
                             start=(n == 0), stop=(n == 15))
                for n in range(16):
                    ins = e.matmul(pf[5][:, 32 * jg : 32 * jg + 32],
                                   k_s[:, 256 * n + 128 * g : 256 * n + 128 * g + 128],
                                   ones32[:, 0:32], start=(n == 0), stop=(n == 15))
            p.inc(ins, "tS")
            p.mark(f"bkv_vk_{s}")

    def wchain_bkv(e, p, j):
        RB = 8 + 36 * (j % 2)
        p.wait(e, f"a_exp_{j}", "aS")
        p.D(e, e.tensor_scalar(stats[:, RB + 28 : RB + 29], stats[:, RB + 27 : RB + 28],
                               scst[:, C_ONE : C_ONE + 1],
                               scst[:, C_ONE : C_ONE + 1], ALU.mult, ALU.add))
        p.D(e, e.reciprocal(stats[:, RB + 29 : RB + 30], stats[:, RB + 28 : RB + 29]))
        p.D(e, e.tensor_tensor(stats[:, RB + 30 : RB + 31], stats[:, RB + 27 : RB + 28],
                               stats[:, RB + 29 : RB + 30], ALU.mult))
        p.D(e, e.tensor_scalar(sfw[:, 8 * j : 8 * j + 8], stats[:, RB + 8 : RB + 16],
                               stats[:, RB + 29 : RB + 30],
                               scst[:, C_ZERO : C_ZERO + 1], ALU.mult, ALU.add))
        p.D(e, e.scalar_tensor_tensor(sfw[:, 8 * j : 8 * j + 8], stats[:, RB + 16 : RB + 24],
                                      stats[:, RB + 30 : RB + 31],
                                      sfw[:, 8 * j : 8 * j + 8], ALU.mult, ALU.add))
        p.inc(e.nop(), "vS")
        p.mark(f"a_top2_{j}")

    def Bkv_scalar(e, p):
        def expd(j_):
            RB = 8 + 36 * (j_ % 2)
            p.wait(e, f"a_d_{j_}", "vS")
            e.activation(stats[:, RB + 27 : RB + 28], stats[:, RB + 26 : RB + 27], ACT.Exp)
            p.adrain(e)
            p.mark(f"a_exp_{j_}")

        for s in range(8):
            for n in range(16):
                p.wait(e, f"bkv_kmm_{s}_{n}", "tS")
                p.inc(e.activation(k_s[:, 256 * n : 256 * (n + 1)], pf[n % 2][:, 0:256], ACT.Relu),
                      "aS")
                p.mark(f"bkv_kev_{s}_{n}")
            if s >= 1:
                expd(2 * s - 2)
                expd(2 * s - 1)
        expd(14)
        expd(15)

    def Bkv_vector(e, p):
        p.wait(e, f"a_rt_{NT-1}", "tS")
        p.inc(e.tensor_copy(lgall[:, :], pf[0][:, 0:128]), "vS")
        p.mark("lg_copied")
        for s in range(8):
            for n in range(16):
                p.wait(e, f"bkv_vmm_{s}_{n}", "tS")
                p.inc(e.tensor_copy(v_s[:, 256 * n : 256 * (n + 1)], pf[2 + n % 2][:, 0:256]), "vS")
                p.mark(f"bkv_vev_{s}_{n}")
            p.wait(e, f"bkv_vk_{s}", "tS")
            for g in range(2):
                jg = 2 * s + g
                for i in range(4):
                    e.tensor_copy(vkbd[32 * i : 32 * i + 32, 128 * jg + 32 * i : 128 * jg + 32 * i + 32],
                                  pf[4][32 * i : 32 * i + 32, 128 * g + 32 * i : 128 * g + 32 * i + 32])
                    ins = e.tensor_copy(sdenm[32 * i : 32 * i + 32, 4 * jg + i : 4 * jg + i + 1],
                                        pf[5][32 * i : 32 * i + 32, 32 * jg : 32 * jg + 1])
            p.inc(ins, "vS")
            p.mark(f"bkv_vke_{s}")
            if s >= 1:
                wchain_bkv(e, p, 2 * s - 2)
                wchain_bkv(e, p, 2 * s - 1)
            for j in (2 * s, 2 * s + 1):
                RB = 8 + 36 * (j % 2)
                p.D(e, e.tensor_tensor(stats[:, RB : RB + 8], lgall[:, 8 * j : 8 * j + 8],
                                       srbb[:, 0:8], ALU.add))
                p.D(e, e.reduce_max(stats[:, RB + 24 : RB + 25], stats[:, RB : RB + 8], axis=AX))
                p.D(e, e.tensor_scalar(stats[:, RB + 8 : RB + 16], stats[:, RB : RB + 8],
                                       stats[:, RB + 24 : RB + 25],
                                       scst[:, C_ONE : C_ONE + 1], ALU.is_equal, ALU.mult))
                p.D(e, e.scalar_tensor_tensor(stats[:, RB + 16 : RB + 24], stats[:, RB + 8 : RB + 16],
                                              scst[:, C_N1E30 : C_N1E30 + 1],
                                              stats[:, RB : RB + 8], ALU.mult, ALU.add))
                p.D(e, e.reduce_max(stats[:, RB + 25 : RB + 26], stats[:, RB + 16 : RB + 24], axis=AX))
                p.D(e, e.tensor_scalar(stats[:, RB + 16 : RB + 24], stats[:, RB + 16 : RB + 24],
                                       stats[:, RB + 25 : RB + 26],
                                       scst[:, C_ONE : C_ONE + 1], ALU.is_equal, ALU.mult))
                p.inc(e.tensor_tensor(stats[:, RB + 26 : RB + 27], stats[:, RB + 25 : RB + 26],
                                      stats[:, RB + 24 : RB + 25], ALU.subtract), "vS")
                p.mark(f"a_d_{j}")
        wchain_bkv(e, p, 14)
        wchain_bkv(e, p, 15)

    def Batt_gp(e, p):
        for j in range(2, NT):
            if j >= 4:
                p.wait(e, f"batt_tp_{j-2}", f"dT{j % 2}")
            p.inc(e.dma_start(out=wqnat[:, 2048 * (j % 2) : 2048 * (j % 2) + 2048],
                              in_=wqkv_h[128 * j : 128 * (j + 1), :]), f"dW{j % 2}", 16)
            p.mark(f"batt_wq_{j}")
        p.wait(e, f"batt_amix_{4*NT-1}", "vS")
        p.inc(e.dma_start(out=sbpjb[:, :], in_=bpt_h[:, :]), "dW3", 16)
        p.mark("batt_bp")
        for cs in range(2, 8):
            if cs >= 3:
                p.wait(e, f"batt_wptp_{cs-1}", "dT2")
            src = wproj_h[256 * cs : 256 * (cs + 1), :].rearrange("(oc p) c -> p oc c", p=128)
            dst = wstg[:, :].rearrange("p (oc c) -> p oc c", oc=2)
            p.inc(e.dma_start(out=dst, in_=src), "dW2", 16)
            p.mark(f"batt_wp_{cs}")

    def Batt_sync(e, p):
        for j in range(2, NT):
            p.wait(e, f"batt_wq_{j}", f"dW{j % 2}")
            if j >= 4:
                p.wait(e, f"batt_qmm_{j-2}_3", "tS")
            src = wqnat[:, 2048 * (j % 2) : 2048 * (j % 2) + 2048]
            dst = wqT[:, 2048 * (j % 2) : 2048 * (j % 2) + 2048]
            dst = dst.rearrange("p (cc m) -> p cc m", m=128)
            p.inc(e.dma_start_transpose(out=dst, in_=src), f"dT{j % 2}", 16)
            p.mark(f"batt_tp_{j}")
        for cs in range(8):
            if cs >= 2:
                p.wait(e, f"batt_wp_{cs}", "dW2")
                p.wait(e, f"batt_pmm_{cs-2}_15", "tS")
                wt = wpT0 if cs % 2 == 0 else wpT1
                for oc in range(2):
                    src = wstg[:, 2048 * oc : 2048 * (oc + 1)]
                    dst = wt[:, :].rearrange("p (cc o) -> p cc o", o=256)[:, :, 128 * oc : 128 * oc + 128]
                    p.inc(e.dma_start_transpose(out=dst, in_=src), "dT2", 16)
                p.mark(f"batt_wptp_{cs}")
            for n in range(16):
                pi = 16 * cs + n
                p.wait(e, f"batt_o1_{pi}", "vS")
                p.inc(e.dma_start(out=out_h[128 * n : 128 * (n + 1), 256 * cs : 256 * (cs + 1)],
                                  in_=oev[:, 256 * (pi % 4) : 256 * (pi % 4) + 256]),
                      f"dX{2 + pi % 2}", 16)
                p.mark(f"batt_o1w_{pi}")

    def Batt_tensor(e, p):
        def q_mm(j):
            if j < 2:
                p.wait(e, "batt_tp_1", "dT4")
            else:
                p.wait(e, f"batt_tp_{j}", f"dT{j % 2}")
            wqsrc = big[:, 16384 : 16384 + 4096] if j < 2 else wqT
            wqoff = 2048 * (j % 2) if j >= 2 else 2048 * j
            for ns in range(4):
                qi = 4 * j + ns
                if qi >= 2:
                    p.wait(e, f"batt_qev_{qi-2}", "aS")
                for h in range(2):
                    for cc in range(NCH):
                        ins = e.matmul(pf[qi % 2][:, 256 * h : 256 * (h + 1)],
                                       wqsrc[:, wqoff + 128 * cc : wqoff + 128 * cc + 128],
                                       xn[:, cc * 2048 + 512 * ns + 256 * h : cc * 2048 + 512 * ns + 256 * (h + 1)],
                                       start=(cc == 0), stop=(cc == 15))
                p.inc(ins, "tS")
                p.mark(f"batt_qmm_{j}_{ns}")

        def den_mm(j, ns_range):
            qt = qb[:, 2048 * (j % 2) : 2048 * (j % 2) + 2048]
            for ns in ns_range:
                mi = 4 * j + ns
                p.wait(e, f"batt_qev_{mi}", "aS")
                if mi >= 2:
                    p.wait(e, f"batt_ln_{mi-2}", "aS")
                dbank = pf[4 if ns % 2 == 0 else 6]
                for h in range(2):
                    ins = e.matmul(dbank[0:4, 256 * h : 256 * (h + 1)], sdenm[:, 4 * j : 4 * j + 4],
                                   qt[:, 512 * ns + 256 * h : 512 * ns + 256 * (h + 1)],
                                   start=True, stop=True)
                p.inc(ins, "tS")
                p.mark(f"batt_den_{mi}")

        def numbc(j):
            qt = qb[:, 2048 * (j % 2) : 2048 * (j % 2) + 2048]
            for ns in range(4):
                mi = 4 * j + ns
                if mi >= 2:
                    p.wait(e, f"batt_amix_{mi-2}", "vS")
                for h in range(2):
                    ins = e.matmul(pf[2 + mi % 2][:, 256 * h : 256 * (h + 1)],
                                   vkbd[:, 128 * j : 128 * (j + 1)],
                                   qt[:, 512 * ns + 256 * h : 512 * ns + 256 * (h + 1)],
                                   start=True, stop=True)
                p.inc(ins, "tS")
                p.mark(f"batt_num_{mi}")
                p.wait(e, f"batt_exp_{mi}", "aS")
                if mi == 0:
                    p.wait(e, "batt_rden0", "vS")
                if mi >= 2:
                    p.wait(e, f"batt_bce_{mi-2}", "aS")
                bbank = pf[5 if ns % 2 == 0 else 7]
                for h in range(2):
                    ins = e.matmul(bbank[:, 256 * h : 256 * (h + 1)], ind32[0:32, :],
                                   rden[0:32, 512 * (mi % 4) + 256 * h : 512 * (mi % 4) + 256 * (h + 1)],
                                   start=True, stop=True)
                p.inc(ins, "tS")
                p.mark(f"batt_bc_{mi}")

        q_mm(0)
        den_mm(0, (0, 1))
        den_mm(0, (2, 3))
        for j in range(1, NT):
            q_mm(j)
            den_mm(j, (0, 1))
            numbc(j - 1)
            den_mm(j, (2, 3))
        numbc(NT - 1)

        p.wait(e, f"batt_amix_{4*NT-1}", "vS")
        for cs in range(8):
            wt = wpT0 if cs % 2 == 0 else wpT1
            if cs >= 2:
                p.wait(e, f"batt_wptp_{cs}", "dT2")
            for n in range(16):
                pi = 16 * cs + n
                if pi >= 4:
                    p.wait(e, f"batt_o1_{pi-4}", "vS")
                for cc in range(NCH):
                    ins = e.matmul(pf[pi % 4][:, 0:256],
                                   big[:, cc * 2048 + 128 * n : cc * 2048 + 128 * n + 128],
                                   wt[:, 256 * cc : 256 * (cc + 1)], start=(cc == 0), stop=(cc == 15))
                p.inc(ins, "tS")
                p.mark(f"batt_pmm_{cs}_{n}")

    def Batt_scalar(e, p):
        def qev(j):
            for ns in range(4):
                qi = 4 * j + ns
                p.wait(e, f"batt_qmm_{j}_{ns}", "tS")
                p.inc(e.activation(qb[:, 2048 * (j % 2) + 512 * ns : 2048 * (j % 2) + 512 * (ns + 1)],
                                   pf[qi % 2][:, 0:512], ACT.Relu), "aS")
                p.mark(f"batt_qev_{qi}")

        def lnexp(j, ns_range):
            for ns in ns_range:
                mi = 4 * j + ns
                dbank = pf[4 if ns % 2 == 0 else 6]
                p.wait(e, f"batt_den_{mi}", "tS")
                if mi == 0:
                    p.wait(e, "batt_rden0", "vS")
                if mi >= 4:
                    p.wait(e, f"batt_bc_{mi-4}", "tS")
                p.inc(e.activation(rden[0:4, 512 * (mi % 4) : 512 * (mi % 4) + 512],
                                   dbank[0:4, 0:512],
                                   ACT.Ln, bias=scst[0:4, C_1E15 : C_1E15 + 1]), "aS")
                p.mark(f"batt_ln_{mi}")
                if p.emit:
                    e.drain()
                e.activation(rden[0:4, 512 * (mi % 4) : 512 * (mi % 4) + 512],
                             rden[0:4, 512 * (mi % 4) : 512 * (mi % 4) + 512],
                             ACT.Exp, scale=scst[0:4, C_NEG1 : C_NEG1 + 1])
                p.adrain(e)
                p.mark(f"batt_exp_{mi}")

        def bce(j):
            for ns in range(4):
                mi = 4 * j + ns
                bbank = pf[5 if ns % 2 == 0 else 7]
                p.wait(e, f"batt_bc_{mi}", "tS")
                p.inc(e.activation(bcsb[:, 512 * (mi % 2) : 512 * (mi % 2) + 512], bbank[:, 0:512],
                                   ACT.Copy), "aS")
                p.mark(f"batt_bce_{mi}")

        qev(0)
        lnexp(0, (0, 1))
        lnexp(0, (2, 3))
        for j in range(1, NT):
            qev(j)
            lnexp(j, (0, 1))
            bce(j - 1)
            lnexp(j, (2, 3))
        bce(NT - 1)

    def Batt_vector(e, p):
        p.inc(e.memset(rden[:, :], 0.0), "vS")
        p.mark("batt_rden0")

        def amix(j):
            for ns in range(4):
                mi = 4 * j + ns
                p.wait(e, f"batt_num_{mi}", "tS")
                p.wait(e, f"batt_bce_{mi}", "aS")
                p.inc(e.tensor_tensor(big[:, j * 2048 + 512 * ns : j * 2048 + 512 * (ns + 1)],
                                      pf[2 + mi % 2][:, 0:512],
                                      bcsb[:, 512 * (mi % 2) : 512 * (mi % 2) + 512], ALU.mult), "vS")
                p.mark(f"batt_amix_{mi}")

        for j in range(NT):
            amix(j)
        p.wait(e, "batt_bp", "dW3")
        for cs in range(8):
            for n in range(16):
                pi = 16 * cs + n
                p.wait(e, f"batt_pmm_{cs}_{n}", "tS")
                if pi >= 2:
                    p.wait(e, f"batt_o1w_{pi-2}", f"dX{2 + pi % 2}")
                p.inc(e.tensor_tensor(oev[:, 256 * (pi % 4) : 256 * (pi % 4) + 256],
                                      pf[pi % 4][:, 0:256],
                                      sbpjb[:, 256 * cs : 256 * (cs + 1)], ALU.add), "vS")
                p.mark(f"batt_o1_{pi}")

    def C1_gp(e, p):
        for ee in range(NE):
            if ee >= 2:
                p.wait(e, f"c1_mm_{ee-2}_3_15", "tS")
            src = ebt_h[ee, :, :]
            dst = ebb[:, 2048 * (ee % 2) : 2048 * (ee % 2) + 2048]
            p.inc(e.dma_start(out=dst, in_=src), f"dW{ee % 2}", 16)
            p.mark(f"c1_eb_{ee}")
            for fs in range(4):
                qi = 4 * ee + fs
                if qi >= 2:
                    pe_, pfs = divmod(qi - 2, 4)
                    p.wait(e, f"c1_mm_{pe_}_{pfs}_15", "tS")
                src = ew_h[ee, :, 512 * fs : 512 * (fs + 1)].rearrange("(j p) f -> p j f", p=128)
                dst = eq[:, 8192 * (qi % 2) : 8192 * (qi % 2 + 1)].rearrange("p (j f) -> p j f", j=16)
                p.inc(e.dma_start(out=dst, in_=src), f"dW{2 + qi % 2}", 16)
                p.mark(f"c1_q_{ee}_{fs}")

    def C1_tensor(e, p):
        for ee in range(NE):
            p.wait(e, f"c1_eb_{ee}", f"dW{ee % 2}")
            for fs in range(4):
                qi = 4 * ee + fs
                p.wait(e, f"c1_q_{ee}_{fs}", f"dW{2 + qi % 2}")
                buf = eq[:, 8192 * (qi % 2) : 8192 * (qi % 2 + 1)]
                for t in range(NT):
                    gi = 16 * qi + t
                    if gi >= 6:
                        p.wait(e, f"c1_hb_{gi-6}", "vS")
                    for h in range(2):
                        for dc in range(NCH):
                            ins = e.matmul(pf[gi % 6][:, 256 * h : 256 * (h + 1)],
                                           xn[:, dc * 2048 + 128 * t : dc * 2048 + 128 * t + 128],
                                           buf[:, 512 * dc + 256 * h : 512 * dc + 256 * (h + 1)],
                                           start=(dc == 0), stop=(dc == 15))
                    p.inc(ins, "tS")
                    p.mark(f"c1_mm_{ee}_{fs}_{t}")

    def C1_vector(e, p):
        def mac(gi):
            qi2, t2 = divmod(gi, NT)
            ee2, fs2 = divmod(qi2, 4)
            p.wait(e, f"c1_silu_{gi}", "aS")
            dst = big[:, t2 * 2048 + 512 * fs2 : t2 * 2048 + 512 * (fs2 + 1)]
            sl = silu[:, 512 * (gi % 2) : 512 * (gi % 2) + 512]
            fwc = sfw[:, 8 * t2 + ee2 : 8 * t2 + ee2 + 1]
            if ee2 == 0:
                p.inc(e.tensor_scalar(dst, sl, fwc, scst[:, C_ZERO : C_ZERO + 1],
                                      ALU.mult, ALU.add), "vS")
            else:
                p.inc(e.scalar_tensor_tensor(dst, sl, fwc, dst, ALU.mult, ALU.add), "vS")
            p.mark(f"c1_mac_{gi}")

        for gi in range(NE * 4 * NT):
            qi, t = divmod(gi, NT)
            ee, fs = divmod(qi, 4)
            p.wait(e, f"c1_mm_{ee}_{fs}_{t}", "tS")
            if gi >= 2:
                p.wait(e, f"c1_silu_{gi-2}", "aS")
            p.inc(e.tensor_tensor(hbuf[:, 512 * (gi % 2) : 512 * (gi % 2) + 512],
                                  pf[gi % 6][:, 0:512],
                                  ebb[:, 2048 * (ee % 2) + 512 * fs : 2048 * (ee % 2) + 512 * (fs + 1)],
                                  ALU.add), "vS")
            p.mark(f"c1_hb_{gi}")
            if gi >= 1:
                mac(gi - 1)
        mac(NE * 4 * NT - 1)

    def C1_scalar(e, p):
        for gi in range(NE * 4 * NT):
            p.wait(e, f"c1_hb_{gi}", "vS")
            if gi >= 2:
                p.wait(e, f"c1_mac_{gi-2}", "vS")
            p.inc(e.activation(silu[:, 512 * (gi % 2) : 512 * (gi % 2) + 512],
                               hbuf[:, 512 * (gi % 2) : 512 * (gi % 2) + 512], ACT.Silu), "aS")
            p.mark(f"c1_silu_{gi}")

    def C2_gp(e, p):
        p.inc(e.dma_start(out=cpad_s[:, :], in_=cpad_h[:, :]), "dW4", 16)
        p.mark("c2_cpad")
        p.wait(e, "c1_mm_7_2_15", "tS")
        p.inc(e.dma_start(out=pbb[:, :], in_=pbt_h[:, :]), "dW0", 16)
        p.inc(e.dma_start(out=snwb[:, :], in_=nwt_h[:, :]), "dW0", 16)
        p.mark("c2_nw")
        for s in range(4):
            p.wait(e, f"c2_tp_{4*s+3}", f"dT{s}")
            src = pw_h[:, 512 * s : 512 * (s + 1)].rearrange("(j p) f -> p j f", p=128)
            dst = pwq[:, 8192 * s : 8192 * (s + 1)].rearrange("p (j f) -> p j f", j=16)
            p.inc(e.dma_start(out=dst, in_=src), "dW1", 16)
            p.mark(f"c2_pw_{s}")

    def C2_gp2(e, p):
        for t in range(NT):
            if t >= 1:
                p.wait(e, f"c2_o1x_{t-1}", "vS")
            p.inc(e.dma_start(out=xst[:, 0:2048], in_=x_h[128 * t : 128 * (t + 1), :]), "dW3", 16)
            p.mark(f"c2_x_{t}")

    def C1_sync(e, p):
        for tj in range(NT):
            p.wait(e, f"c1_mac_{496 + tj}", "vS")
            p.wait(e, f"c1_mm_7_3_{tj}", "tS")
            src = big[:, 2048 * tj : 2048 * (tj + 1)]
            dst = combT[:, :].rearrange("p (fc t) -> p fc t", t=2048)[:, :, 128 * tj : 128 * tj + 128]
            p.inc(e.dma_start_transpose(out=dst, in_=src), f"dT{tj // 4}", 16)
            p.mark(f"c2_tp_{tj}")

    def C2_sync(e, p):
        for t in range(NT):
            if t >= 1:
                p.wait(e, f"c2_w_{t-1}", "dX2")
                p.wait(e, f"c2_w_{t-1}", "dX3")
            p.inc(e.dma_start(out=o1rd[:, 0:1024], in_=out_h[128 * t : 128 * (t + 1), 0:1024]),
                  "dX1", 16)
            p.inc(e.dma_start(out=o1rd[:, 1024:2048], in_=out_h[128 * t : 128 * (t + 1), 1024:2048]),
                  "dX2", 16)
            p.mark(f"c2_o1_{t}")
            p.wait(e, f"c2_fin_{t}", "vS")
            p.inc(e.dma_start(out=out_h[128 * t : 128 * (t + 1), 0:1024], in_=o1rd[:, 0:1024]),
                  "dX2", 16)
            p.inc(e.dma_start(out=out_h[128 * t : 128 * (t + 1), 1024:2048], in_=o1rd[:, 1024:2048]),
                  "dX3", 16)
            p.mark(f"c2_w_{t}")
        if p.emit:
            for s_ in DMA_SEMS:
                if p.cnt[s_] > 0:
                    e.wait_ge(p.sems[s_], p.cnt[s_])

    def C2_tensor(e, p):
        p.wait(e, "c2_pw_3", "dW1")
        p.wait(e, "c2_cpad", "dW4")
        p.wait(e, "c2_nw", "dW0")
        for t in range(NT):
            p.wait(e, f"c2_tp_{4 * (t // 4) + 3}", f"dT{t // 4}")
            if t >= 2:
                p.wait(e, f"c2_yadd_{t-2}", "vS")
            pq = [pf[4 * (t % 2) + s] for s in range(4)]
            for h in range(2):
                for fc in range(NCH):
                    for s in range(4):
                        e.matmul(pq[s][:, 256 * h : 256 * (h + 1)],
                                 combT[:, fc * 2048 + 128 * t : fc * 2048 + 128 * t + 128],
                                 pwq[:, 8192 * s + 512 * fc + 256 * h : 8192 * s + 512 * fc + 256 * (h + 1)],
                                 start=(fc == 0), stop=False)
                for s in range(4):
                    ins = e.matmul(pq[s][:, 256 * h : 256 * (h + 1)], cpad_s[:, :],
                                   pbb[:, 512 * s + 256 * h : 512 * s + 256 * h + 256],
                                   start=False, stop=True)
            p.inc(ins, "tS")
            p.mark(f"c2_mm_{t}")

    def C2_vector(e, p):
        p.wait(e, "c2_nw", "dW0")
        for t in range(NT):
            p.wait(e, f"c2_mm_{t}", "tS")
            p.wait(e, f"c2_x_{t}", "dW3")
            for s in range(4):
                p.D(e, e.tensor_scalar(ybuf[:, 512 * s : 512 * (s + 1)], xst[:, 512 * s : 512 * (s + 1)],
                                       srstd[:, t : t + 1], snegmr[:, t : t + 1], ALU.mult, ALU.add))
                tt = e.tensor_tensor(ybuf[:, 512 * s : 512 * (s + 1)], ybuf[:, 512 * s : 512 * (s + 1)],
                                     pf[4 * (t % 2) + s][:, 0:512], ALU.add)
                if s < 3:
                    p.D(e, tt)
                else:
                    p.inc(tt, "vS")
            p.mark(f"c2_yadd_{t}")
            p.mark(f"c2_y_{t}")
            p.wait(e, f"c2_o1_{t}", "dX1")
            p.wait(e, f"c2_o1_{t}", "dX2")
            p.inc(e.tensor_tensor(o1rd[:, :], o1rd[:, :], xst[:, 0:2048], ALU.add), "vS")
            p.mark(f"c2_o1x_{t}")
            p.wait(e, f"c2_rms_{t}", "aS")
            p.D(e, e.scalar_tensor_tensor(ybuf[:, :], ybuf[:, :], stats[:, 2:3], snwb[:, :],
                                          ALU.mult, ALU.mult))
            p.inc(e.tensor_tensor(o1rd[:, :], ybuf[:, :], o1rd[:, :], ALU.add), "vS")
            p.mark(f"c2_fin_{t}")

    def C2_scalar(e, p):
        for t in range(NT):
            p.wait(e, f"c2_y_{t}", "vS")
            p.inc(e.activation(sdump2[:, :], ybuf[:, :], ACT.Square, accum_out=stats[:, 1:2]), "aS")
            p.mark(f"c2_ssq_{t}")
            if p.emit:
                e.drain()
            e.activation(stats[:, 3:4], stats[:, 1:2], ACT.Ln, scale=scst[:, C_I2048 : C_I2048 + 1],
                         bias=scst[:, C_EPSRMS : C_EPSRMS + 1])
            if p.emit:
                e.drain()
            e.activation(stats[:, 2:3], stats[:, 3:4], ACT.Exp, scale=scst[:, C_NEG05 : C_NEG05 + 1])
            p.adrain(e)
            p.mark(f"c2_rms_{t}")

    phases = []

    def add(engine, fn):
        phases.append((engine, fn))

    def add_barrier(tag, light=()):
        def end(sem):
            def f(e, p):
                if p.emit:
                    e.drain()
                p.inc(e.nop(), sem)
                p.mark(f"bar_{tag}_{sem}")
            return f

        def wt(skip_dma):
            def f(e, p):
                for s in ("tS", "vS", "aS"):
                    p.wait(e, f"bar_{tag}_{s}", s)
                if p.emit and not skip_dma:
                    for s in DMA_SEMS:
                        v = p.val(f"bar_{tag}_tS", s)
                        if v > 0:
                            e.wait_ge(p.sems[s], v)
            return f

        add("vector", end("vS"))
        add("scalar", end("aS"))
        add("tensor", end("tS"))
        for eng in ("tensor", "vector", "scalar", "gpsimd", "sync"):
            add(eng, wt(eng in light))

    add("sync", setup_sync)
    add("gpsimd", setup_gp)
    add("vector", setup_vec)
    add_barrier("s")
    if upto >= 2:
        add("sync", A_sync)
        add("scalar", A_scalar)
        add("vector", A_vector)
        add("tensor", A_tensor)
        add_barrier("a")
    if upto >= 3:
        add("gpsimd", Bkv_gp)
        add("sync", Bkv_sync)
        add("tensor", Bkv_tensor)
        add("scalar", Bkv_scalar)
        add("vector", Bkv_vector)
        add_barrier("bk")
    if upto >= 4:
        add("gpsimd", Batt_gp)
        add("sync", Batt_sync)
        add("tensor", Batt_tensor)
        add("scalar", Batt_scalar)
        add("vector", Batt_vector)
        add_barrier("ba")
    if upto >= 5:
        add("gpsimd", C1_gp)
        add("tensor", C1_tensor)
        add("scalar", C1_scalar)
        add("vector", C1_vector)
        if upto >= 6:
            add("sync", C1_sync)
            add("gpsimd", C2_gp)
        add_barrier("c1", light=("tensor", "vector", "scalar", "sync") if upto >= 6 else ())
    if upto >= 6:
        add("gpsimd", C2_gp2)
        add("sync", C2_sync)
        add("tensor", C2_tensor)
        add("vector", C2_vector)
        add("scalar", C2_scalar)

    if upto < 6:
        def finisher(e, p):
            if p.emit:
                for s_ in DMA_SEMS:
                    if p.cnt[s_] > 0:
                        e.wait_ge(p.sems[s_], p.cnt[s_])
        add("sync", finisher)

    p = Ctx()
    mock = MockEng()
    for eng, fn in phases:
        fn(mock, p)
    totals = dict(p.cnt)

    p.emit = True
    p.sems = sems
    for s in SEMS:
        p.cnt[s] = 0
    with nc.Block() as block:
        for eng_name, fn in phases:
            getattr(block, eng_name)(lambda e, fn=fn: fn(e, p))
    assert p.cnt == totals, (p.cnt, totals)

    ctx.close()
    return nc


def make_in_maps(L0_attn_x, L0_attn_w_qkv, L0_attn_w_proj, L0_attn_b_proj,
                 L0_ffn_router_w, L0_ffn_router_b, L0_ffn_expert_w, L0_ffn_expert_b,
                 L0_ffn_proj_w, L0_ffn_proj_b, L0_ffn_norm_w):
    ind32_np = np.zeros((32, 128), np.float32)
    for i in range(4):
        ind32_np[i, 32 * i : 32 * (i + 1)] = 1.0
    cpad_np = np.zeros((128, 128), np.float32)
    cpad_np[0, :] = 1.0
    pbrow = np.zeros((128, C), np.float32)
    pbrow[0, :] = np.asarray(L0_ffn_proj_b, dtype=np.float32)

    def tile(a):
        a = np.asarray(a, dtype=np.float32).reshape(1, -1)
        return np.ascontiguousarray(np.broadcast_to(a, (128, a.shape[1])))

    common = {
        "wqkv": np.ascontiguousarray(L0_attn_w_qkv, dtype=np.float32),
        "wproj": np.ascontiguousarray(L0_attn_w_proj, dtype=np.float32),
        "bpt": tile(L0_attn_b_proj),
        "rw": np.ascontiguousarray(L0_ffn_router_w, dtype=np.float32),
        "rbt": tile(L0_ffn_router_b),
        "ew": np.ascontiguousarray(L0_ffn_expert_w, dtype=np.float32),
        "ebt": np.ascontiguousarray(
            np.broadcast_to(np.asarray(L0_ffn_expert_b, dtype=np.float32)[:, None, :],
                            (NE, 128, C))),
        "pw": np.ascontiguousarray(L0_ffn_proj_w, dtype=np.float32),
        "pbt": pbrow,
        "nwt": tile(L0_ffn_norm_w),
        "ind32": ind32_np,
        "cpad": cpad_np,
    }
    in_maps = []
    for b in range(8):
        m = dict(common)
        m["x"] = np.ascontiguousarray(L0_attn_x[b], dtype=np.float32)
        in_maps.append(m)
    return in_maps


def kernel(L0_attn_x, L0_attn_w_qkv, L0_attn_w_proj, L0_attn_b_proj,
           L0_ffn_router_w, L0_ffn_router_b, L0_ffn_expert_w, L0_ffn_expert_b,
           L0_ffn_proj_w, L0_ffn_proj_b, L0_ffn_norm_w):
    nc = build_program()
    in_maps = make_in_maps(L0_attn_x, L0_attn_w_qkv, L0_attn_w_proj, L0_attn_b_proj,
                           L0_ffn_router_w, L0_ffn_router_b, L0_ffn_expert_w, L0_ffn_expert_b,
                           L0_ffn_proj_w, L0_ffn_proj_b, L0_ffn_norm_w)
    res = run_bass_kernel_spmd(nc, in_maps, core_ids=list(range(8)))
    return np.stack([np.asarray(r["out"]) for r in res.results], axis=0).astype(np.float32)
